# version 11
# baseline (speedup 1.0000x reference)
import numpy as np
import ml_dtypes
from contextlib import ExitStack

import concourse.bass as bass
import concourse.mybir as mybir
from concourse.bass_utils import run_bass_kernel_spmd
F32 = mybir.dt.float32
BF16 = mybir.dt.bfloat16
AX = mybir.AxisListType
ALU = mybir.AluOpType
ACT = mybir.ActivationFunctionType

D = 1024
DH = 2048
IFF = 2816
GROUPS4 = [[0, 1, 2, 3], [4, 5, 6, 7]]
GROUP8 = [list(range(8))]
WSH = 384
FFN_BLOCKS = [(r * WSH, min(WSH, IFF - r * WSH)) for r in range(8)]
NIC = IFF // 128


class Sem:
    def __init__(self, ctx, nc, name):
        self.h = ctx.enter_context(nc.semaphore(name))
        self.n = 0

    def inc(self, inst, k=1):
        inst.then_inc(self.h, k)
        self.n += k

    def dma(self, inst):
        inst.then_inc(self.h, 16)
        self.n += 16

    def wait(self, eng, val=None):
        eng.wait_ge(self.h, self.n if val is None else val)


def build_nc(T):
    NT = T // 128
    SUP = 256
    NSUP = T // SUP
    TPS = SUP // 128
    SUB = T // 4
    TPG = SUB // 128
    QB = max(128, T // 8)
    NQS = SUB // QB
    NSS = QB // 128
    NQB_OWN = T // QB
    KJ = T // 128
    NTB = T // 512

    nc = bass.Bass("TRN2", target_bir_lowering=False, debug=False,
                   detect_race_conditions=False)

    x_in = nc.dram_tensor("x_own", [T, D], F32, kind="ExternalInput").ap()
    wqT_d = nc.dram_tensor("wqT", [D, DH], F32, kind="ExternalInput").ap()
    wkT_d = nc.dram_tensor("wkT", [D, DH], F32, kind="ExternalInput").ap()
    wvT_d = nc.dram_tensor("wvT", [D, DH], F32, kind="ExternalInput").ap()
    woT_d = nc.dram_tensor("woT", [DH, D], F32, kind="ExternalInput").ap()
    bq_in = nc.dram_tensor("bq", [128, 16], F32, kind="ExternalInput").ap()
    bk_in = nc.dram_tensor("bk", [128, 16], F32, kind="ExternalInput").ap()
    bv_in = nc.dram_tensor("bv", [1, DH], F32, kind="ExternalInput").ap()
    bo_in = nc.dram_tensor("bo", [1, D], F32, kind="ExternalInput").ap()
    gsh_in = nc.dram_tensor("gate_sh", [WSH, D], F32, kind="ExternalInput").ap()
    ush_in = nc.dram_tensor("up_sh", [WSH, D], F32, kind="ExternalInput").ap()
    dsh_in = nc.dram_tensor("down_sh", [D, WSH], F32, kind="ExternalInput").ap()
    id_in = nc.dram_tensor("ident", [128, 128], BF16, kind="ExternalInput").ap()
    out_d = nc.dram_tensor("out", [T, D], F32, kind="ExternalOutput").ap()

    wg_in = nc.dram_tensor("wg_in", [D, WSH], BF16)
    wu_in = nc.dram_tensor("wu_in", [D, WSH], BF16)
    wd_in = nc.dram_tensor("wd_in", [WSH, D], BF16)
    wg_full = nc.dram_tensor("wg_full", [8 * D, WSH], BF16)
    wu_full = nc.dram_tensor("wu_full", [8 * D, WSH], BF16)
    wd_full = nc.dram_tensor("wd_full", [8 * WSH, D], BF16)
    qt_in = nc.dram_tensor("qt_in", [NQB_OWN * 16 * 128, QB], BF16)
    qt_full = nc.dram_tensor("qt_full", [4 * NQB_OWN * 16 * 128, QB], BF16)
    nrm_in = nc.dram_tensor("nrm_in", [1, 2], F32)
    nrm_out = nc.dram_tensor("nrm_out", [4, 2], F32)
    opart = [nc.dram_tensor(f"opart{g}", [4 * SUB, 2052], F32) for g in range(4)]
    ors = [nc.dram_tensor(f"ors{g}", [SUB, 2052], F32) for g in range(4)]
    x1_sp = nc.dram_tensor("x1_sp", [T, D], F32)
    x2_sp = nc.dram_tensor("x2_sp", [T, D], F32)
    ln2_sp = nc.dram_tensor("ln2_sp", [8, 128, T], BF16)

    st0 = ExitStack()
    with st0:
        sb0 = lambda n, s, d: st0.enter_context(nc.sbuf_tensor(n, s, d))
        ident = sb0("identt", [128, 128], BF16)
        ones_r = sb0("ones_r", [1, 128], BF16)
        ones_c = sb0("ones_c", [128, 1], BF16)
        bq_sb = sb0("bq_sb", [128, 16], F32)
        bk_sb = sb0("bk_sb", [128, 16], F32)
        bv_sb = sb0("bv_sb", [1, DH], BF16)
        bo_sb = sb0("bo_sb", [1, D], BF16)
        negm = sb0("negm", [128, 1], F32)
        mrow_b = sb0("mrow_b", [1, 128], BF16)
        ones_q = sb0("ones_q", [1, 512], BF16)
        eps_t = sb0("eps_t", [128, 1], F32)
        nrm_sb = sb0("nrm_sb", [1, 8], F32)
        nrm_pk = sb0("nrm_pk", [1, 2], F32)
        q2m = sb0("q2m", [1, 1], F32)
        k2m = sb0("k2m", [1, 1], F32)
        t11 = sb0("t11", [1, 1], F32)
        t11b = sb0("t11b", [1, 1], BF16)
        red1 = sb0("red1", [128, 1], F32)
        red2 = sb0("red2", [128, 1], F32)
        mm2_t = sb0("mm2_t", [128, 2], F32)
        cmm2_t = sb0("cmm2_t", [128, 2], F32)
        var_t = sb0("var_t", [128, 1], F32)
        rstd_t = sb0("rstd_t", [128, 1], F32)
        ln_sc = sb0("ln_sc", [128, 1024], F32)
        ln_bf = sb0("ln_bf", [128, 1024], BF16)
        si_sc = sb0("si_sc", [128, WSH], F32)
        tstage = sb0("tstage", [128, 1024], BF16)

        qbuf = {}

        def alloc_quant(stack, wmax, tag):
            sbq = lambda n, s, d: stack.enter_context(
                nc.sbuf_tensor(n + tag, s, d))
            rnb = wmax // 16
            qbuf["RNB"] = rnb
            qbuf["qs_u"] = sbq("qs_u", [128, wmax], F32)
            qbuf["qs_g"] = sbq("qs_g", [128, wmax], BF16)
            qbuf["qb1"] = sbq("qb1", [128, wmax], BF16)
            qbuf["qb2"] = sbq("qb2", [128, wmax], BF16)
            qbuf["qb3"] = sbq("qb3", [128, wmax], BF16)
            qbuf["m_sc"] = sbq("m_sc", [128, rnb], F32)
            qbuf["sc_sc"] = sbq("sc_sc", [128, 2 * rnb], F32)
            qbuf["csr_sc"] = sbq("csr_sc", [128, 2 * rnb], F32)
            qbuf["hq"] = sbq("hq", [128, wmax], BF16)

        ps_wide = st0.enter_context(nc.psum_tensor("ps_wide", [128, 2048], F32))
        ps_a = st0.enter_context(nc.psum_tensor("ps_a", [128, 512], F32))
        ps_b = st0.enter_context(nc.psum_tensor("ps_b", [128, 512], F32))
        ps_z = st0.enter_context(nc.psum_tensor("ps_z", [128, 512], F32))

        names = ["ld", "ldg", "st", "cc", "sq", "var2", "rstd", "sgn", "t2d",
                 "q", "tp", "te", "gu", "si", "hmul", "dn", "x1", "wld",
                 "wld2", "vmm", "vev", "kmm", "kev", "qmm", "qev", "onm",
                 "zn", "nev", "nmr", "qld", "sT", "exp", "av", "oev",
                 "ost0", "ost1", "qst0", "qst1", "rld", "wo", "outd", "ph",
                 "scl", "sclc", "vv"]
        SM = {nm: Sem(st0, nc, nm) for nm in names}
        block = st0.enter_context(nc.Block())

        def V(f): block.vector(f)
        def Sc(f): block.scalar(f)
        def Tn(f): block.tensor(f)
        def Sy(f): block.sync(f)
        def G(f): block.gpsimd(f)

        PH = SM["ph"]
        phase_no = [0]

        def phase_barrier():
            phase_no[0] += 1
            tgt = 5 * phase_no[0]
            @V
            def _(v):
                v.sem_inc(PH.h, 1)
                v.wait_ge(PH.h, tgt)
            @Sc
            def _(s):
                s.sem_inc(PH.h, 1)
                s.wait_ge(PH.h, tgt)
            @Tn
            def _(t):
                t.sem_inc(PH.h, 1)
                t.wait_ge(PH.h, tgt)
            @Sy
            def _(s):
                s.sem_inc(PH.h, 1)
                s.wait_ge(PH.h, tgt)
            @G
            def _(g):
                g.sem_inc(PH.h, 1)
                g.wait_ge(PH.h, tgt)
            PH.n = tgt

        def emit_ln(src_ap, dst_ap, sc_waits=()):
            @V
            def _(v):
                SM["vv"].wait(v)
                for nm, val in sc_waits:
                    SM[nm].wait(v, val)
                i = v.reduce_sum(red1[:], src_ap, AX.X)
                SM["vv"].inc(i)
            @G
            def _(g):
                SM["q"].wait(g)
                for nm, val in sc_waits:
                    SM[nm].wait(g, val)
                i = g.tensor_mul(qbuf["qs_u"][:, 0:1024], src_ap, src_ap)
                SM["sq"].inc(i)
            @V
            def _(v):
                SM["vv"].wait(v)
                SM["sq"].wait(v)
                i = v.reduce_sum(red2[:], qbuf["qs_u"][:, 0:1024], AX.X)
                SM["vv"].inc(i)
                i = v.tensor_scalar_mul(mm2_t[:, 0:1], red1[:], 1.0 / D)
                SM["vv"].inc(i)
            @V
            def _(v):
                SM["vv"].wait(v)
                i = v.tensor_mul(mm2_t[:, 1:2], mm2_t[:, 0:1], mm2_t[:, 0:1])
                SM["vv"].inc(i)
            @V
            def _(v):
                SM["vv"].wait(v)
                i = v.scalar_tensor_tensor(var_t[:], red2[:], 1.0 / D,
                                           mm2_t[:, 1:2],
                                           ALU.mult, ALU.subtract)
                SM["var2"].inc(i)
            @Sc
            def _(s):
                SM["var2"].wait(s)
                i = s.activation(rstd_t[:], var_t[:], ACT.Sqrt,
                                 bias=eps_t[:, 0:1], scale=1.0)
                SM["rstd"].inc(i)
            @V
            def _(v):
                SM["vv"].wait(v)
                SM["rstd"].wait(v)
                i = v.reciprocal(mm2_t[:, 1:2], rstd_t[:])
                SM["vv"].inc(i)
            @G
            def _(g):
                SM["vv"].wait(g)
                i = g.tensor_copy(cmm2_t[:], mm2_t[:])
                SM["sclc"].inc(i)
            @V
            def _(v):
                SM["vv"].wait(v)
                SM["sclc"].wait(v)
                i = v.tensor_scalar(dst_ap, src_ap, cmm2_t[:, 0:1],
                                    cmm2_t[:, 1:2], ALU.subtract, ALU.mult)
                SM["q"].inc(i)

        def emit_quant(src_ap, out_ap, W, sc_waits=(), v_waits=()):
            nb = W // 16
            rnb = qbuf["RNB"]
            qs_u, qs_g = qbuf["qs_u"], qbuf["qs_g"]
            m_sc, sc_sc, csr_sc = qbuf["m_sc"], qbuf["sc_sc"], qbuf["csr_sc"]
            src3 = src_ap.rearrange("p (s n) -> p s n", n=16)
            @Sc
            def _(s):
                for nm, val in sc_waits:
                    SM[nm].wait(s, val)
                SM["q"].wait(s)
                i = s.sign(qs_g[:, 0:W], src_ap)
                SM["sgn"].inc(i)
                i = s.activation(qs_u[:, 0:W], src_ap, ACT.Abs)
                SM["sgn"].inc(i)
            u3 = qs_u[:, 0:W].rearrange("p (s n) -> p s n", n=16)
            u = qs_u[:, 0:W]
            b1, b2, b3 = (qbuf["qb1"][:, 0:W], qbuf["qb2"][:, 0:W],
                          qbuf["qb3"][:, 0:W])

            def vsec(*ops):
                waits, fns = [], []
                for o in ops:
                    (waits if isinstance(o, tuple) else fns).append(o)
                @V
                def _(v):
                    SM["vv"].wait(v)
                    for nm, val in waits:
                        SM[nm].wait(v, val)
                    for fn in fns:
                        i = fn(v)
                        SM["vv"].inc(i)

            vsec(*v_waits,
                 lambda v: v.tensor_reduce(m_sc[:, 0:nb], src3, AX.X, ALU.max,
                                           apply_absolute_value=True))
            vsec(lambda v: v.tensor_scalar(sc_sc[:, 0:nb], m_sc[:, 0:nb],
                                           1.0 / 6.0, 1e-12, ALU.mult,
                                           ALU.max))
            vsec(lambda v: v.reciprocal(sc_sc[:, rnb:rnb + nb], sc_sc[:, 0:nb]))
            @G
            def _(g):
                SM["vv"].wait(g)
                i = g.tensor_copy(csr_sc[:, 0:nb], sc_sc[:, 0:nb])
                SM["sclc"].inc(i)
                i = g.tensor_copy(csr_sc[:, rnb:rnb + nb],
                                  sc_sc[:, rnb:rnb + nb])
                SM["sclc"].inc(i)
            rb = csr_sc[:, rnb:rnb + nb].unsqueeze(2).broadcast_to(
                [128, nb, 16])
            scb = csr_sc[:, 0:nb].unsqueeze(2).broadcast_to([128, nb, 16])
            vsec(("sclc", None), ("te", None), ("sgn", None),
                 lambda v: v.tensor_tensor(u3, u3, rb, ALU.mult))
            vsec(lambda v: v.tensor_scalar(b1, u, 0.25, 0.5, ALU.is_gt, ALU.mult),
                 lambda v: v.tensor_scalar(b2, u, 0.75, 0.5, ALU.is_gt, ALU.mult),
                 lambda v: v.tensor_scalar(b3, u, 1.25, 0.5, ALU.is_gt, ALU.mult),
                 lambda v: v.tensor_scalar(out_ap, u, 1.75, 0.5, ALU.is_gt, ALU.mult))
            vsec(lambda v: v.tensor_add(b1, b1, b2),
                 lambda v: v.tensor_add(b3, b3, out_ap))
            vsec(lambda v: v.tensor_scalar(b2, u, 2.5, 1.0, ALU.is_gt, ALU.mult),
                 lambda v: v.tensor_scalar(out_ap, u, 3.5, 1.0, ALU.is_gt, ALU.mult))
            vsec(lambda v: v.tensor_add(b1, b1, b3),
                 lambda v: v.tensor_add(b2, b2, out_ap))
            vsec(lambda v: v.tensor_scalar(b3, u, 5.0, 2.0, ALU.is_gt, ALU.mult),
                 lambda v: v.tensor_add(b1, b1, b2))
            vsec(lambda v: v.tensor_add(b1, b1, b3))
            vsec(lambda v: v.tensor_mul(u, b1, qs_g[:, 0:W]))
            @V
            def _(v):
                SM["vv"].wait(v)
                SM["sclc"].wait(v)
                i = v.tensor_tensor(
                    out_ap.rearrange("p (s n) -> p s n", n=16),
                    u3, scb, ALU.mult)
                SM["q"].inc(i)

        def emit_transpose(src_ap, col0, nchunk, dst_ap, tn_waits=()):
            @Sy
            def _(sy):
                SM["q"].wait(sy)
                for nm, val in tn_waits:
                    SM[nm].wait(sy, val)
                for c in range(nchunk):
                    SM["te"].dma(sy.dma_start_transpose(
                        out=dst_ap[:, c, :],
                        in_=src_ap[:, col0 + c * 128:col0 + (c + 1) * 128]))

        @Sy
        def _(sy):
            SM["ld"].dma(sy.dma_start(out=ident[:], in_=id_in))
            SM["ld"].dma(sy.dma_start(out=bq_sb[:], in_=bq_in))
            SM["ld"].dma(sy.dma_start(out=bk_sb[:], in_=bk_in))
        @G
        def _(g):
            g.memset(ones_r[:], 1.0)
            g.memset(ones_c[:], 1.0)
            g.memset(ones_q[:], 1.0)
            SM["ldg"].dma(g.dma_start(out=bv_sb[:], in_=bv_in))
            SM["ldg"].dma(g.dma_start(out=bo_sb[:], in_=bo_in))
        @V
        def _(v):
            v.memset(q2m[:], 0.0)
            v.memset(k2m[:], 0.0)
            v.memset(eps_t[:], 1e-5)

        stW = ExitStack()
        alloc_quant(stW, 1024, "W")
        wsh_g = stW.enter_context(nc.sbuf_tensor("wsh_g", [128, 3 * 1024], F32))
        wsh_u = stW.enter_context(nc.sbuf_tensor("wsh_u", [128, 3 * 1024], F32))
        wsh_d = stW.enter_context(nc.sbuf_tensor("wsh_d", [128, 8 * WSH], F32))
        wq_bf = stW.enter_context(nc.sbuf_tensor("wq_bf", [128, 1024], BF16))
        @Sy
        def _(sy):
            SM["ld"].dma(sy.dma_start(
                out=wsh_g[:].rearrange("p (t d) -> p t d", t=3),
                in_=gsh_in.rearrange("(t p) d -> p t d", p=128)))
            SM["ld"].dma(sy.dma_start(
                out=wsh_u[:].rearrange("p (t d) -> p t d", t=3),
                in_=ush_in.rearrange("(t p) d -> p t d", p=128)))
            SM["ld"].dma(sy.dma_start(
                out=wsh_d[:].rearrange("p (t w) -> p t w", t=8),
                in_=dsh_in.rearrange("(t p) w -> p t w", p=128)))
        @V
        def _(v):
            SM["ld"].wait(v)
        @Sc
        def _(s):
            SM["ld"].wait(s)
            SM["ldg"].wait(s)
        @Tn
        def _(t):
            SM["ld"].wait(t)
            SM["ldg"].wait(t)

        def w_qts(src_sb, tile_i, W, dst_dram_ap):
            nchunk = W // 128
            src = src_sb[:, tile_i * W:(tile_i + 1) * W]
            emit_quant(src, wq_bf[:, 0:W], W)
            for h in range(0, nchunk, 4):
                nch = min(4, nchunk - h)
                emit_transpose(
                    wq_bf[:, 0:W], h * 128, nch,
                    tstage[:, 0:nch * 128].rearrange("p (c x) -> p c x", c=nch))
                @Sy
                def _(sy, h=h, nch=nch):
                    SM["te"].wait(sy)
                    SM["st"].dma(sy.dma_start(
                        out=dst_dram_ap[h:h + nch].transpose([1, 0, 2]),
                        in_=tstage[:, 0:nch * 128]
                        .rearrange("p (c x) -> p c x", c=nch)))
                @Sc
                def _(s):
                    SM["st"].wait(s)

        for src_sb, bounce in ((wsh_g, wg_in), (wsh_u, wu_in)):
            for t3 in range(3):
                dst = bounce.ap().rearrange(
                    "(c p) (t x) -> c p t x", p=128, x=128)[:, :, t3, :]
                w_qts(src_sb, t3, 1024, dst)
        for t8 in range(8):
            dst = wd_in.ap().rearrange(
                "(c p) (t x) -> c p t x", p=128, x=128)[:, :, t8, :]
            w_qts(wsh_d, t8, WSH, dst)

        @G
        def _(g):
            SM["st"].wait(g)
            for a, b in ((wg_in, wg_full), (wu_in, wu_full), (wd_in, wd_full)):
                i = g.collective_compute(
                    "AllGather", ALU.bypass, replica_groups=GROUP8,
                    ins=[a.ap().opt()], outs=[b.ap().opt()])
                SM["cc"].inc(i)
        CC_W = SM["cc"].n

        stW.close()
        phase_barrier()

        wstate = {"wfree": [], "wdfree": []}

        def emit_ffn(x_sb, ntile, base_tok, bufs, ln2_spill, out_write,
                     x_ready):
            STW = ntile * 128
            h_full = bufs["h"]
            xqT = bufs["xqT"][:].rearrange("p (c t) -> p c t", c=8)
            hqT = bufs["hqT"][:].rearrange("p (c t) -> p c t", c=NIC)
            wg3 = bufs["wg"][:].rearrange("p (b c w) -> p b c w", b=2, c=8)
            wu3 = bufs["wu"][:].rearrange("p (b c w) -> p b c w", b=2, c=8)
            wd3 = bufs["wd"][:].rearrange("p (c d) -> p c d", c=NIC)

            for tl in range(ntile):
                xt = x_sb[:, tl * 1024:(tl + 1) * 1024]
                emit_ln(xt, ln_sc[:], sc_waits=x_ready)
                emit_quant(ln_sc[:], ln_bf[:], 1024,
                           v_waits=(("q", SM["q"].n),))
                for h in (0, 4):
                    emit_transpose(ln_bf[:], h * 128, 4,
                                   xqT[:, h:h + 4, tl * 128:(tl + 1) * 128])

            @Sy
            def _(sy):
                sy.wait_ge(SM["cc"].h, CC_W)
                if wstate["wdfree"]:
                    SM["dn"].wait(sy, wstate["wdfree"][-1])
                SM["wld2"].dma(sy.dma_start(
                    out=wd3,
                    in_=wd_full.ap().rearrange(
                        "(c p) d -> p c d", p=128)[:, 0:NIC, :]))

            hm_last = [0] * ntile
            for r, (off, w) in enumerate(FFN_BLOCKS):
                par = len(wstate["wfree"]) % 2
                @Sy
                def _(sy, r=r, par=par, w=w):
                    sy.wait_ge(SM["cc"].h, CC_W)
                    hist = wstate["wfree"]
                    if len(hist) >= 2:
                        SM["gu"].wait(sy, hist[-2])
                    SM["wld"].dma(sy.dma_start(
                        out=wg3[:, par, :, 0:w],
                        in_=wg_full.ap().rearrange(
                            "(r c p) w -> r p c w", c=8, p=128)[r][:, :, 0:w]))
                    SM["wld"].dma(sy.dma_start(
                        out=wu3[:, par, :, 0:w],
                        in_=wu_full.ap().rearrange(
                            "(r c p) w -> r p c w", c=8, p=128)[r][:, :, 0:w]))
                for tl in range(ntile):
                    @Tn
                    def _(t, par=par, w=w, tl=tl):
                        SM["wld"].wait(t)
                        SM["te"].wait(t)
                        SM["si"].wait(t)
                        SM["hmul"].wait(t)
                        i = None
                        for fc in range(8):
                            i = t.matmul(ps_a[:, 0:w],
                                         xqT[:, fc, tl * 128:(tl + 1) * 128],
                                         wg3[:, par, fc, 0:w],
                                         start=(fc == 0), stop=(fc == 7))
                        SM["gu"].inc(i)
                        for fc in range(8):
                            i = t.matmul(ps_b[:, 0:w],
                                         xqT[:, fc, tl * 128:(tl + 1) * 128],
                                         wu3[:, par, fc, 0:w],
                                         start=(fc == 0), stop=(fc == 7))
                        SM["gu"].inc(i)
                    @Sc
                    def _(s, w=w):
                        SM["gu"].wait(s)
                        i = s.activation(si_sc[:, 0:w], ps_a[:, 0:w], ACT.Silu)
                        SM["si"].inc(i)
                    @V
                    def _(v, w=w, off=off, tl=tl):
                        SM["si"].wait(v)
                        i = v.tensor_tensor(
                            h_full[:, tl * IFF + off:tl * IFF + off + w],
                            si_sc[:, 0:w], ps_b[:, 0:w], ALU.mult)
                        SM["hmul"].inc(i)
                    hm_last[tl] = SM["hmul"].n
                wstate["wfree"].append(SM["gu"].n)

            for tl in range(ntile):
                emit_quant(h_full[:, tl * IFF:(tl + 1) * IFF],
                           qbuf["hq"][:, 0:IFF], IFF,
                           sc_waits=(("hmul", hm_last[tl]),),
                           v_waits=(("hmul", hm_last[tl]),))
                for hh in range(0, NIC, 4):
                    nch = min(4, NIC - hh)
                    emit_transpose(qbuf["hq"][:, 0:IFF], hh * 128, nch,
                                   hqT[:, hh:hh + nch,
                                       tl * 128:(tl + 1) * 128])

            for tl in range(ntile):
                @Tn
                def _(t, tl=tl):
                    SM["wld2"].wait(t)
                    SM["te"].wait(t)
                    SM["x1"].wait(t)
                    i = None
                    for ic in range(NIC):
                        stat = hqT[:, ic, tl * 128:(tl + 1) * 128]
                        i = t.matmul(ps_wide[:, 0:512], stat, wd3[:, ic, 0:512],
                                     start=(ic == 0), stop=(ic == NIC - 1))
                        i = t.matmul(ps_wide[:, 512:1024], stat,
                                     wd3[:, ic, 512:1024],
                                     start=(ic == 0), stop=(ic == NIC - 1))
                    SM["dn"].inc(i)
                x1_rec = SM["x1"].n + 1
                @V
                def _(v, tl=tl):
                    SM["x1"].wait(v)
                    SM["dn"].wait(v)
                    xt = x_sb[:, tl * 1024:(tl + 1) * 1024]
                    i = v.scalar_tensor_tensor(xt, ps_wide[:, 0:1024], 0.5,
                                               xt, ALU.mult, ALU.add)
                    SM["x1"].inc(i)
                if ln2_spill:
                    xt = x_sb[:, tl * 1024:(tl + 1) * 1024]
                    tok = base_tok + tl * 128
                    emit_ln(xt, ln_bf[:], sc_waits=(("x1", x1_rec),))
                    for h in (0, 4):
                        emit_transpose(
                            ln_bf[:], h * 128, 4,
                            tstage[:, 0:512].rearrange("p (c x) -> p c x", c=4),
                            tn_waits=(("st", SM["st"].n),))
                        @Sy
                        def _(sy, h=h, tok=tok):
                            SM["te"].wait(sy)
                            SM["st"].dma(sy.dma_start(
                                out=ln2_sp.ap()[h:h + 4, :, tok:tok + 128]
                                .transpose([1, 0, 2]),
                                in_=tstage[:, 0:512]
                                .rearrange("p (c x) -> p c x", c=4)))
                    @Sy
                    def _(sy, tl=tl, tok=tok):
                        SM["x1"].wait(sy)
                        SM["st"].dma(sy.dma_start(
                            out=x1_sp.ap()[tok:tok + 128, :],
                            in_=x_sb[:, tl * 1024:(tl + 1) * 1024]))
                if out_write is not None:
                    out_write(tl)
            wstate["wdfree"].append(SM["dn"].n)

        stF1 = ExitStack()
        sbF = lambda n, s, d: stF1.enter_context(nc.sbuf_tensor(n, s, d))
        alloc_quant(stF1, IFF, "F")
        x_st = sbF("x_st", [128, TPS * 1024], F32)
        f1bufs = {
            "xqT": sbF("xqT1", [128, 8 * SUP], BF16),
            "hqT": sbF("hqT1", [128, NIC * SUP], BF16),
            "h": sbF("h1", [128, TPS * IFF], BF16),
            "wg": sbF("wg1", [128, 2 * 8 * WSH], BF16),
            "wu": sbF("wu1", [128, 2 * 8 * WSH], BF16),
            "wd": sbF("wd1", [128, NIC * 1024], BF16),
        }
        for sti in range(NSUP):
            @Sy
            def _(sy, sti=sti):
                if sti > 0:
                    SM["x1"].wait(sy)
                    SM["sq"].wait(sy)
                    SM["st"].wait(sy)
                SM["ld"].dma(sy.dma_start(
                    out=x_st[:].rearrange("p (t d) -> p t d", t=TPS),
                    in_=x_in.rearrange("(t p) d -> p t d", p=128)
                    [:, sti * TPS:(sti + 1) * TPS, :]))
            ld_rec = SM["ld"].n
            emit_ffn(x_st[:], TPS, sti * SUP, f1bufs, ln2_spill=True,
                     out_write=None, x_ready=(("ld", ld_rec),))

        stF1.close()
        phase_barrier()

        stP = ExitStack()
        sbP = lambda n, s, d: stP.enter_context(nc.sbuf_tensor(n, s, d))
        K_sb = sbP("K_sb", [128, 16 * T], BF16)
        V_sb = sbP("V_sb", [128, KJ * 2048], BF16)
        K3 = K_sb[:].rearrange("p (c k) -> p c k", c=16)
        V3 = V_sb[:].rearrange("p (j d) -> p j d", j=KJ)
        stP3 = ExitStack()
        sbP3 = lambda n, s, d: stP3.enter_context(nc.sbuf_tensor(n, s, d))
        aw_sb = sbP3("aw_sb", [128, 8 * 1024], BF16)
        aw3 = aw_sb[:].rearrange("p (c d) -> p c d", c=8)
        l2b = sbP3("l2b", [128, 8 * 512], BF16)
        l2b3 = l2b[:].rearrange("p (c t) -> p c t", c=8)
        qstage = sbP3("qstage", [128, 2 * 512], BF16)
        sqst = sbP3("sqst", [128, 512], BF16)
        q2part = sbP3("q2part", [1, T], F32)
        k2part = sbP3("k2part", [1, T], F32)

        def load_ln2(tb, waits):
            @Sy
            def _(sy, tb=tb):
                for nm, val in waits:
                    SM[nm].wait(sy, val)
                SM["ld"].dma(sy.dma_start(
                    out=l2b3,
                    in_=ln2_sp.ap()[:, :, tb * 512:(tb + 1) * 512]
                    .transpose([1, 0, 2])))

        def load_awhalf(src_dram, half, waits):
            @G
            def _(g, half=half):
                for nm, val in waits:
                    SM[nm].wait(g, val)
                SM["ldg"].dma(g.dma_start(
                    out=aw3,
                    in_=src_dram.rearrange("(c p) e -> p c e", p=128)
                    [:, :, half * 1024:(half + 1) * 1024]))

        qcnt = [0]
        for half in range(2):
            load_awhalf(wqT_d, half,
                        [("qmm", SM["qmm"].n)] if half else [])
            for tb in range(NTB):
                load_ln2(tb, [("qmm", SM["qmm"].n)] if (half or tb) else [])
                for dch in range(8):
                    dc = half * 8 + dch
                    pp = ps_a if (dch % 2 == 0) else ps_b
                    par = qcnt[0] % 2
                    qcnt[0] += 1
                    @Tn
                    def _(t, dch=dch, pp=pp):
                        SM["ld"].wait(t)
                        SM["ldg"].wait(t)
                        SM["qev"].wait(t, max(0, SM["qev"].n - 1))
                        i = None
                        for fc in range(8):
                            i = t.matmul(pp[:, 0:512],
                                         aw3[:, fc, dch * 128:dch * 128 + 128],
                                         l2b3[:, fc, :],
                                         start=(fc == 0), stop=(fc == 7))
                        SM["qmm"].inc(i)
                    @Sc
                    def _(s, dc=dc, pp=pp, par=par):
                        SM["qmm"].wait(s)
                        SM[f"qst{par}"].wait(s)
                        i = s.activation(qstage[:, par * 512:(par + 1) * 512],
                                         pp[:, 0:512], ACT.Identity,
                                         bias=bq_sb[:, dc:dc + 1], scale=1.0)
                        SM["qev"].inc(i)
                    @V
                    def _(v, par=par):
                        SM["qev"].wait(v)
                        i = v.tensor_mul(sqst[:],
                                         qstage[:, par * 512:(par + 1) * 512],
                                         qstage[:, par * 512:(par + 1) * 512])
                        SM["onm"].inc(i)
                    @Tn
                    def _(t, dch=dch):
                        SM["onm"].wait(t)
                        if dch == 0:
                            SM["nev"].wait(t)
                        i = t.matmul(ps_z[0:1, 0:512], ones_c[:], sqst[:],
                                     start=(dch == 0), stop=(dch == 7))
                        if dch == 7:
                            SM["zn"].inc(i)
                    @Sy
                    def _(sy, tb=tb, dc=dc, par=par):
                        SM["qev"].wait(sy)
                        for s2 in range(512 // QB):
                            qb = (tb * 512) // QB + s2
                            SM[f"qst{par}"].dma(sy.dma_start(
                                out=qt_in.ap()[qb * 2048 + dc * 128:
                                               qb * 2048 + dc * 128 + 128, :],
                                in_=qstage[:, par * 512 + s2 * QB:
                                           par * 512 + (s2 + 1) * QB]))
                if half == 0:
                    @V
                    def _(v, tb=tb):
                        SM["nev"].wait(v)
                        SM["zn"].wait(v)
                        i = v.tensor_copy(q2part[0:1, tb * 512:(tb + 1) * 512],
                                          ps_z[0:1, 0:512])
                        SM["nev"].inc(i)
                else:
                    @V
                    def _(v, tb=tb):
                        SM["nev"].wait(v)
                        SM["zn"].wait(v)
                        i = v.tensor_add(q2part[0:1, tb * 512:(tb + 1) * 512],
                                         q2part[0:1, tb * 512:(tb + 1) * 512],
                                         ps_z[0:1, 0:512])
                        SM["nev"].inc(i)
                    @V
                    def _(v, tb=tb):
                        SM["nev"].wait(v)
                        i = v.reduce_max(t11[:],
                                         q2part[0:1, tb * 512:(tb + 1) * 512],
                                         AX.X)
                        SM["nev"].inc(i)
                    @V
                    def _(v):
                        SM["nev"].wait(v)
                        i = v.tensor_max(q2m[:], q2m[:], t11[:])
                        SM["nev"].inc(i)

        @G
        def _(g):
            SM["qst0"].wait(g)
            SM["qst1"].wait(g)
            i = g.collective_compute(
                "AllGather", ALU.bypass, replica_groups=GROUPS4,
                ins=[qt_in.ap().opt()], outs=[qt_full.ap().opt()])
            SM["cc"].inc(i)
        CC_QT = SM["cc"].n

        for half in range(2):
            load_awhalf(wkT_d, half, [("kmm", SM["kmm"].n)] if half
                        else [("qmm", SM["qmm"].n)])
            for kb in range(NTB):
                load_ln2(kb, [("kmm", SM["kmm"].n)] if (half or kb)
                         else [("qmm", SM["qmm"].n)])
                for dch in range(8):
                    dc = half * 8 + dch
                    pp = ps_a if (dch % 2 == 0) else ps_b
                    @Tn
                    def _(t, dch=dch, pp=pp):
                        SM["ld"].wait(t)
                        SM["ldg"].wait(t)
                        SM["kev"].wait(t, max(0, SM["kev"].n - 1))
                        i = None
                        for fc in range(8):
                            i = t.matmul(pp[:, 0:512],
                                         aw3[:, fc, dch * 128:dch * 128 + 128],
                                         l2b3[:, fc, :],
                                         start=(fc == 0), stop=(fc == 7))
                        SM["kmm"].inc(i)
                    @Sc
                    def _(s, dc=dc, kb=kb, pp=pp):
                        SM["kmm"].wait(s)
                        i = s.activation(K3[:, dc, kb * 512:(kb + 1) * 512],
                                         pp[:, 0:512], ACT.Identity,
                                         bias=bk_sb[:, dc:dc + 1], scale=1.0)
                        SM["kev"].inc(i)
                    @V
                    def _(v, dc=dc, kb=kb):
                        SM["kev"].wait(v)
                        i = v.tensor_mul(sqst[:],
                                         K3[:, dc, kb * 512:(kb + 1) * 512],
                                         K3[:, dc, kb * 512:(kb + 1) * 512])
                        SM["onm"].inc(i)
                    @Tn
                    def _(t, dch=dch):
                        SM["onm"].wait(t)
                        if dch == 0:
                            SM["nev"].wait(t)
                        i = t.matmul(ps_z[0:1, 0:512], ones_c[:], sqst[:],
                                     start=(dch == 0), stop=(dch == 7))
                        if dch == 7:
                            SM["zn"].inc(i)
                if half == 0:
                    @V
                    def _(v, kb=kb):
                        SM["nev"].wait(v)
                        SM["zn"].wait(v)
                        i = v.tensor_copy(k2part[0:1, kb * 512:(kb + 1) * 512],
                                          ps_z[0:1, 0:512])
                        SM["nev"].inc(i)
                else:
                    @V
                    def _(v, kb=kb):
                        SM["nev"].wait(v)
                        SM["zn"].wait(v)
                        i = v.tensor_add(k2part[0:1, kb * 512:(kb + 1) * 512],
                                         k2part[0:1, kb * 512:(kb + 1) * 512],
                                         ps_z[0:1, 0:512])
                        SM["nev"].inc(i)
                    @V
                    def _(v, kb=kb):
                        SM["nev"].wait(v)
                        i = v.reduce_max(t11[:],
                                         k2part[0:1, kb * 512:(kb + 1) * 512],
                                         AX.X)
                        SM["nev"].inc(i)
                    @V
                    def _(v):
                        SM["nev"].wait(v)
                        i = v.tensor_max(k2m[:], k2m[:], t11[:])
                        SM["nev"].inc(i)

        @V
        def _(v):
            SM["nev"].wait(v)
            i = v.tensor_copy(nrm_pk[0:1, 0:1], q2m[:])
            SM["onm"].inc(i)
            i = v.tensor_copy(nrm_pk[0:1, 1:2], k2m[:])
            SM["onm"].inc(i)
        @Sy
        def _(sy):
            SM["onm"].wait(sy)
            SM["st"].dma(sy.dma_start(out=nrm_in.ap(), in_=nrm_pk[:]))
        @G
        def _(g):
            SM["st"].wait(g)
            i = g.collective_compute(
                "AllGather", ALU.bypass, replica_groups=GROUPS4,
                ins=[nrm_in.ap().opt()], outs=[nrm_out.ap().opt()])
            SM["cc"].inc(i)
        CC_NRM = SM["cc"].n
        @Sy
        def _(sy):
            sy.wait_ge(SM["cc"].h, CC_NRM)
            SM["ld"].dma(sy.dma_start(
                out=nrm_sb[:], in_=nrm_out.ap().rearrange("a b -> (a b)").unsqueeze(0)))
        @V
        def _(v):
            SM["vv"].wait(v)
            SM["ld"].wait(v)
            i = v.tensor_reduce(nrm_pk[0:1, 0:2],
                                nrm_sb[:].rearrange("p (a b) -> p b a", b=2),
                                AX.X, ALU.max)
            SM["vv"].inc(i)
        @V
        def _(v):
            SM["vv"].wait(v)
            i = v.tensor_mul(t11[:], nrm_pk[0:1, 0:1], nrm_pk[0:1, 1:2])
            SM["onm"].inc(i)
        @Sc
        def _(s):
            SM["onm"].wait(s)
            i = s.activation(rstd_t[0:1, 0:1], t11[:], ACT.Sqrt)
            SM["rstd"].inc(i)
        @V
        def _(v):
            SM["rstd"].wait(v)
            i = v.tensor_scalar_mul(t11b[:], rstd_t[0:1, 0:1], -1.0)
            SM["onm"].inc(i)
        @Tn
        def _(t):
            SM["onm"].wait(t)
            SM["nev"].wait(t)
            i = t.matmul(ps_z[0:1, 0:128], t11b[:], ones_r[:],
                         start=True, stop=True)
            SM["zn"].inc(i)
        @V
        def _(v):
            SM["zn"].wait(v)
            i = v.tensor_copy(mrow_b[:], ps_z[0:1, 0:128])
            SM["nmr"].inc(i)
        NMR = SM["nmr"].n

        for half in range(2):
            load_awhalf(wvT_d, half, [("vmm", SM["vmm"].n)] if half
                        else [("kmm", SM["kmm"].n)])
            for tb in range(NTB):
                load_ln2(tb, [("vmm", SM["vmm"].n)] if (half or tb)
                         else [("kmm", SM["kmm"].n)])
                for tw in range(4):
                    tl = tb * 4 + tw
                    @Tn
                    def _(t, half=half, tw=tw):
                        SM["ld"].wait(t)
                        SM["ldg"].wait(t)
                        SM["vev"].wait(t)
                        i = None
                        for n in range(2):
                            nn = half * 1024 + n * 512
                            for fc in range(8):
                                i = t.matmul(
                                    ps_wide[:, n * 512:(n + 1) * 512],
                                    l2b3[:, fc, tw * 128:(tw + 1) * 128],
                                    aw3[:, fc, n * 512:(n + 1) * 512],
                                    start=(fc == 0), stop=False)
                            i = t.matmul(ps_wide[:, n * 512:(n + 1) * 512],
                                         ones_r[:], bv_sb[:, nn:nn + 512],
                                         start=False, stop=True)
                        SM["vmm"].inc(i)
                    @Sc
                    def _(s, half=half, tl=tl):
                        SM["vmm"].wait(s)
                        i = s.copy(V3[:, tl, half * 1024:(half + 1) * 1024],
                                   ps_wide[:, 0:1024])
                        SM["vev"].inc(i)

        phase_barrier()

        stP3.close()
        qtb = sbP("qtb", [128, 2 * 16 * QB], BF16)
        qtb4 = qtb[:].rearrange("p (b c q) -> p b c q", b=2, c=16)
        ptb = sbP("ptb", [128, KJ * QB], BF16)
        ptb3 = ptb[:].rearrange("p (j q) -> p j q", j=KJ)
        ostage = sbP("ostage", [128, 2 * 2052], F32)

        @V
        def _(v):
            v.memset(ostage[:, 2049:2052], 0.0)
            v.memset(ostage[:, 2052 + 2049:2052 + 2052], 0.0)

        qfree = [0, 0]
        exp_par = [0, 0]
        rs_cc = []
        qn = [0]
        on = [0]
        for g in range(4):
            for o in range(4):
                for qs2 in range(NQS):
                    qb_own = g * NQS + qs2
                    qb_glob = o * NQB_OWN + qb_own
                    par = qn[0] % 2
                    qn[0] += 1
                    @Sy
                    def _(sy, qb_glob=qb_glob, par=par):
                        sy.wait_ge(SM["cc"].h, CC_QT)
                        SM["sT"].wait(sy, qfree[par])
                        SM["qld"].dma(sy.dma_start(
                            out=qtb4[:, par],
                            in_=qt_full.ap().rearrange(
                                "(b c p) q -> b p c q", c=16, p=128)[qb_glob]))
                    for j in range(KJ):
                        pp = ps_a if (j % 2 == 0) else ps_b
                        ppi = j % 2
                        @Tn
                        def _(t, j=j, pp=pp, ppi=ppi, par=par):
                            SM["qld"].wait(t)
                            t.wait_ge(SM["nmr"].h, NMR)
                            SM["exp"].wait(t, exp_par[ppi])
                            for dc in range(16):
                                t.matmul(pp[:, 0:QB],
                                         K3[:, dc, j * 128:(j + 1) * 128],
                                         qtb4[:, par, dc, :],
                                         start=(dc == 0), stop=False)
                            i = t.matmul(pp[:, 0:QB], mrow_b[:, j * 0:128],
                                         ones_q[:, 0:QB],
                                         start=False, stop=True)
                            SM["sT"].inc(i)
                        @Sc
                        def _(s, j=j, pp=pp):
                            SM["sT"].wait(s)
                            i = s.activation(ptb3[:, j, :], pp[:, 0:QB],
                                             ACT.Exp, bias=0.0,
                                             scale=1.0 / 32.0)
                            SM["exp"].inc(i)
                        exp_par[ppi] = SM["exp"].n
                    for ss in range(NSS):
                        opar = on[0] % 2
                        on[0] += 1
                        @Tn
                        def _(t, ss=ss):
                            SM["exp"].wait(t)
                            SM["oev"].wait(t)
                            i = None
                            for j in range(KJ):
                                stat = ptb3[:, j, ss * 128:(ss + 1) * 128]
                                for n in range(4):
                                    t.matmul(ps_wide[:, n * 512:(n + 1) * 512],
                                             stat, V3[:, j, n * 512:(n + 1) * 512],
                                             start=(j == 0), stop=(j == KJ - 1))
                                i = t.matmul(ps_z[:, ss:ss + 1], stat, ones_c[:],
                                             start=(j == 0), stop=(j == KJ - 1))
                            SM["av"].inc(i)
                        @V
                        def _(v, ss=ss, opar=opar):
                            SM["av"].wait(v)
                            SM[f"ost{opar}"].wait(v)
                            i = v.tensor_copy(
                                ostage[:, opar * 2052:opar * 2052 + 2048],
                                ps_wide[:])
                            SM["oev"].inc(i)
                            i = v.tensor_copy(
                                ostage[:, opar * 2052 + 2048:opar * 2052 + 2049],
                                ps_z[:, ss:ss + 1])
                            SM["oev"].inc(i)
                        @Sy
                        def _(sy, g=g, o=o, qs2=qs2, ss=ss, opar=opar):
                            SM["oev"].wait(sy)
                            row0 = o * SUB + qs2 * QB + ss * 128
                            SM[f"ost{opar}"].dma(sy.dma_start(
                                out=opart[g].ap()[row0:row0 + 128, :],
                                in_=ostage[:, opar * 2052:(opar + 1) * 2052]))
                    qfree[par] = SM["sT"].n
            @G
            def _(g_, g=g):
                SM["ost0"].wait(g_)
                SM["ost1"].wait(g_)
                i = g_.collective_compute(
                    "ReduceScatter", ALU.add, replica_groups=GROUPS4,
                    ins=[opart[g].ap().opt()], outs=[ors[g].ap().opt()])
                SM["cc"].inc(i)
            rs_cc.append(SM["cc"].n)

        stP.close()
        phase_barrier()

        stP5 = ExitStack()
        sbQ = lambda n, s, d: stP5.enter_context(nc.sbuf_tensor(n, s, d))
        woT_sb = sbQ("woT_sb", [128, 16 * 1024], BF16)
        woT3 = woT_sb[:].rearrange("p (c d) -> p c d", c=16)
        rsb = sbQ("rsb", [128, 2052], F32)
        x_g = sbQ("x_g", [128, TPG * 1024], F32)
        obf = sbQ("obf", [128, 2048], BF16)
        oT = sbQ("oT", [128, 16 * 128], BF16)
        oT3 = oT[:].rearrange("p (c x) -> p c x", c=16)
        rz_t = sbQ("rz_t", [128, 1], F32)
        @G
        def _(g):
            SM["ldg"].dma(g.dma_start(
                out=woT3, in_=woT_d.rearrange("(c p) d -> p c d", p=128)))
        @Sy
        def _(sy):
            SM["st"].wait(sy)

        for g in range(4):
            for tl in range(TPG):
                tok = g * SUB + tl * 128
                @Sy
                def _(sy, g=g, tl=tl, tok=tok):
                    sy.wait_ge(SM["cc"].h, rs_cc[g])
                    SM["q"].wait(sy)
                    SM["st"].wait(sy)
                    SM["rld"].dma(sy.dma_start(
                        out=rsb[:], in_=ors[g].ap()[tl * 128:(tl + 1) * 128, :]))
                    SM["rld"].dma(sy.dma_start(
                        out=x_g[:, tl * 1024:(tl + 1) * 1024],
                        in_=x1_sp.ap()[tok:tok + 128, :]))
                @V
                def _(v):
                    SM["rld"].wait(v)
                    i = v.reciprocal(rz_t[:], rsb[:, 2048:2049])
                    SM["scl"].inc(i)
                @Sc
                def _(s):
                    SM["scl"].wait(s)
                    i = s.mul(obf[:], rsb[:, 0:2048], rz_t[:, 0:1])
                    SM["q"].inc(i)
                for h in (0, 4, 8, 12):
                    emit_transpose(obf[:], h * 128, 4, oT3[:, h:h + 4, :])
                @Tn
                def _(t, tl=tl):
                    SM["te"].wait(t)
                    SM["ldg"].wait(t)
                    SM["x1"].wait(t)
                    i = None
                    for n in range(2):
                        for dc in range(16):
                            i = t.matmul(ps_wide[:, n * 512:(n + 1) * 512],
                                         oT3[:, dc, :],
                                         woT3[:, dc, n * 512:(n + 1) * 512],
                                         start=(dc == 0), stop=False)
                        i = t.matmul(ps_wide[:, n * 512:(n + 1) * 512],
                                     ones_r[:], bo_sb[:, n * 512:(n + 1) * 512],
                                     start=False, stop=True)
                    SM["wo"].inc(i)
                @V
                def _(v, tl=tl):
                    SM["wo"].wait(v)
                    SM["rld"].wait(v)
                    xt = x_g[:, tl * 1024:(tl + 1) * 1024]
                    i = v.tensor_tensor(xt, ps_wide[:, 0:1024], xt, ALU.add)
                    SM["x1"].inc(i)
                @Sy
                def _(sy, tl=tl, tok=tok):
                    SM["x1"].wait(sy)
                    SM["st"].dma(sy.dma_start(
                        out=x2_sp.ap()[tok:tok + 128, :],
                        in_=x_g[:, tl * 1024:(tl + 1) * 1024]))

        stP5.close()
        phase_barrier()

        stP6 = ExitStack()
        sbG = lambda n, s, d: stP6.enter_context(nc.sbuf_tensor(n, s, d))
        alloc_quant(stP6, IFF, "G")
        x_st2 = sbG("x_st2", [128, TPS * 1024], F32)
        f2bufs = {
            "xqT": sbG("xqT2", [128, 8 * SUP], BF16),
            "hqT": sbG("hqT2", [128, NIC * SUP], BF16),
            "h": sbG("h2", [128, TPS * IFF], BF16),
            "wg": sbG("wg2", [128, 2 * 8 * WSH], BF16),
            "wu": sbG("wu2", [128, 2 * 8 * WSH], BF16),
            "wd": sbG("wd2", [128, NIC * 1024], BF16),
        }
        for sti in range(NSUP):
            @Sy
            def _(sy, sti=sti):
                SM["x1"].wait(sy)
                SM["sq"].wait(sy)
                SM["st"].wait(sy)
                SM["outd"].wait(sy)
                SM["ld"].dma(sy.dma_start(
                    out=x_st2[:].rearrange("p (t d) -> p t d", t=TPS),
                    in_=x2_sp.ap().rearrange("(t p) d -> p t d", p=128)
                    [:, sti * TPS:(sti + 1) * TPS, :]))
            ld_rec = SM["ld"].n

            def out_write(tl, sti=sti):
                @Sy
                def _(sy, tl=tl, sti=sti):
                    SM["x1"].wait(sy)
                    tok = sti * SUP + tl * 128
                    SM["outd"].dma(sy.dma_start(
                        out=out_d[tok:tok + 128, :],
                        in_=x_st2[:, tl * 1024:(tl + 1) * 1024]))

            emit_ffn(x_st2[:], TPS, sti * SUP, f2bufs, ln2_spill=False,
                     out_write=out_write, x_ready=(("ld", ld_rec),))
        stP6.close()

        @Sy
        def _(sy):
            SM["outd"].wait(sy)

    return nc


_NC_CACHE = {}


def _get_nc(T):
    if T not in _NC_CACHE:
        _NC_CACHE[T] = build_nc(T)
    return _NC_CACHE[T]


def make_in_maps(x, Wq, bq, Wk, bk, Wv, bv, Wo, bo, w_gate, w_up, w_down, T):
    f = np.float32
    x = np.ascontiguousarray(np.asarray(x, dtype=f))
    perm = np.concatenate([np.arange(0, DH, 2), np.arange(1, DH, 2)])
    wqT = np.ascontiguousarray(np.asarray(Wq, f)[perm].T)
    wkT = np.ascontiguousarray(np.asarray(Wk, f)[perm].T)
    wvT = np.ascontiguousarray(np.asarray(Wv, f)[perm].T)
    woT = np.ascontiguousarray(np.asarray(Wo, f)[:, perm].T)
    bq_p = np.ascontiguousarray(np.asarray(bq, f)[perm].reshape(16, 128).T)
    bk_p = np.ascontiguousarray(np.asarray(bk, f)[perm].reshape(16, 128).T)
    bv_p = np.ascontiguousarray(np.asarray(bv, f)[perm].reshape(1, DH))
    bo_r = np.ascontiguousarray(np.asarray(bo, f).reshape(1, D))
    wg = np.asarray(w_gate, f)
    wu = np.asarray(w_up, f)
    wd = np.asarray(w_down, f)
    ident = np.eye(128, dtype=ml_dtypes.bfloat16)
    in_maps = []
    for c in range(8):
        seq, sl = c // 4, c % 4
        gsh = np.zeros((WSH, D), f)
        ush = np.zeros((WSH, D), f)
        dsh = np.zeros((D, WSH), f)
        lo = c * WSH
        hi = min((c + 1) * WSH, IFF)
        if hi > lo:
            gsh[0:hi - lo] = wg[lo:hi]
            ush[0:hi - lo] = wu[lo:hi]
            dsh[:, 0:hi - lo] = wd[:, lo:hi]
        in_maps.append({
            "x_own": np.ascontiguousarray(x[seq, sl * T:(sl + 1) * T]),
            "wqT": wqT, "wkT": wkT, "wvT": wvT, "woT": woT,
            "bq": bq_p, "bk": bk_p, "bv": bv_p, "bo": bo_r,
            "gate_sh": gsh, "up_sh": ush, "down_sh": dsh,
            "ident": ident,
        })
    return in_maps


def kernel(L0_attn_hidden_states, L0_attn_Wq, L0_attn_bq, L0_attn_Wk,
           L0_attn_bk, L0_attn_Wv, L0_attn_bv, L0_attn_Wo, L0_attn_bo,
           L0_ffn_w_gate, L0_ffn_w_up, L0_ffn_w_down):
    x = np.asarray(L0_attn_hidden_states, dtype=np.float32)
    B, S, _ = x.shape
    T = S // 4
    nc = _get_nc(T)
    in_maps = make_in_maps(
        x, L0_attn_Wq, L0_attn_bq, L0_attn_Wk, L0_attn_bk, L0_attn_Wv,
        L0_attn_bv, L0_attn_Wo, L0_attn_bo, L0_ffn_w_gate, L0_ffn_w_up,
        L0_ffn_w_down, T)
    res = run_bass_kernel_spmd(nc, in_maps, core_ids=list(range(8)))
    out = np.empty((B, S, D), dtype=np.float32)
    for c in range(8):
        out[c // 4, (c % 4) * T:(c % 4 + 1) * T] = res.results[c]["out"]
    return out


# version 12
# speedup vs baseline: 1.1263x; 1.1263x over previous
import numpy as np
import ml_dtypes
from contextlib import ExitStack

import concourse.bass as bass
import concourse.mybir as mybir
from concourse.bass_utils import run_bass_kernel_spmd
F32 = mybir.dt.float32
BF16 = mybir.dt.bfloat16
AX = mybir.AxisListType
ALU = mybir.AluOpType
ACT = mybir.ActivationFunctionType

D = 1024
DH = 2048
IFF = 2816
GROUPS4 = [[0, 1, 2, 3], [4, 5, 6, 7]]
GROUP8 = [list(range(8))]
WSH = 384
FFN_BLOCKS = [(r * WSH, min(WSH, IFF - r * WSH)) for r in range(8)]
NIC = IFF // 128


class Sem:
    def __init__(self, ctx, nc, name):
        self.h = ctx.enter_context(nc.semaphore(name))
        self.n = 0

    def inc(self, inst, k=1):
        inst.then_inc(self.h, k)
        self.n += k

    def dma(self, inst):
        inst.then_inc(self.h, 16)
        self.n += 16

    def wait(self, eng, val=None):
        eng.wait_ge(self.h, self.n if val is None else val)


def build_nc(T):
    NT = T // 128
    SUP = 256
    NSUP = T // SUP
    TPS = SUP // 128
    SUB = T // 4
    TPG = SUB // 128
    QB = max(128, T // 8)
    NQS = SUB // QB
    NSS = QB // 128
    NQB_OWN = T // QB
    KJ = T // 128
    NTB = T // 512

    nc = bass.Bass("TRN2", target_bir_lowering=False, debug=False,
                   detect_race_conditions=False)

    x_in = nc.dram_tensor("x_own", [T, D], F32, kind="ExternalInput").ap()
    wqT_d = nc.dram_tensor("wqT", [D, DH], F32, kind="ExternalInput").ap()
    wkT_d = nc.dram_tensor("wkT", [D, DH], F32, kind="ExternalInput").ap()
    wvT_d = nc.dram_tensor("wvT", [D, DH], F32, kind="ExternalInput").ap()
    woT_d = nc.dram_tensor("woT", [DH, D], F32, kind="ExternalInput").ap()
    bq_in = nc.dram_tensor("bq", [128, 16], F32, kind="ExternalInput").ap()
    bk_in = nc.dram_tensor("bk", [128, 16], F32, kind="ExternalInput").ap()
    bv_in = nc.dram_tensor("bv", [1, DH], F32, kind="ExternalInput").ap()
    bo_in = nc.dram_tensor("bo", [1, D], F32, kind="ExternalInput").ap()
    gsh_in = nc.dram_tensor("gate_sh", [WSH, D], F32, kind="ExternalInput").ap()
    ush_in = nc.dram_tensor("up_sh", [WSH, D], F32, kind="ExternalInput").ap()
    dsh_in = nc.dram_tensor("down_sh", [D, WSH], F32, kind="ExternalInput").ap()
    id_in = nc.dram_tensor("ident", [128, 128], BF16, kind="ExternalInput").ap()
    out_d = nc.dram_tensor("out", [T, D], F32, kind="ExternalOutput").ap()

    wg_in = nc.dram_tensor("wg_in", [D, WSH], BF16)
    wu_in = nc.dram_tensor("wu_in", [D, WSH], BF16)
    wd_in = nc.dram_tensor("wd_in", [WSH, D], BF16)
    wg_full = nc.dram_tensor("wg_full", [8 * D, WSH], BF16)
    wu_full = nc.dram_tensor("wu_full", [8 * D, WSH], BF16)
    wd_full = nc.dram_tensor("wd_full", [8 * WSH, D], BF16)
    qt_in = nc.dram_tensor("qt_in", [NQB_OWN * 16 * 128, QB], BF16)
    qt_full = nc.dram_tensor("qt_full", [4 * NQB_OWN * 16 * 128, QB], BF16)
    nrm_in = nc.dram_tensor("nrm_in", [1, 2], F32)
    nrm_out = nc.dram_tensor("nrm_out", [4, 2], F32)
    opart = [nc.dram_tensor(f"opart{g}", [4 * SUB, 2052], F32) for g in range(4)]
    ors = [nc.dram_tensor(f"ors{g}", [SUB, 2052], F32) for g in range(4)]
    x1_sp = nc.dram_tensor("x1_sp", [T, D], F32)
    x2_sp = nc.dram_tensor("x2_sp", [T, D], F32)
    ln2_sp = nc.dram_tensor("ln2_sp", [8, 128, T], BF16)

    st0 = ExitStack()
    with st0:
        sb0 = lambda n, s, d: st0.enter_context(nc.sbuf_tensor(n, s, d))
        ident = sb0("identt", [128, 128], BF16)
        ones_r = sb0("ones_r", [1, 128], BF16)
        ones_c = sb0("ones_c", [128, 1], BF16)
        bq_sb = sb0("bq_sb", [128, 16], F32)
        bk_sb = sb0("bk_sb", [128, 16], F32)
        bv_sb = sb0("bv_sb", [1, DH], BF16)
        bo_sb = sb0("bo_sb", [1, D], BF16)
        negm = sb0("negm", [128, 1], F32)
        mrow_b = sb0("mrow_b", [1, 128], BF16)
        ones_q = sb0("ones_q", [1, 512], BF16)
        eps_t = sb0("eps_t", [128, 1], F32)
        nrm_sb = sb0("nrm_sb", [1, 8], F32)
        nrm_pk = sb0("nrm_pk", [1, 2], F32)
        q2m = sb0("q2m", [1, 1], F32)
        k2m = sb0("k2m", [1, 1], F32)
        t11 = sb0("t11", [1, 1], F32)
        t11b = sb0("t11b", [1, 1], BF16)
        red1 = sb0("red1", [128, 1], F32)
        red2 = sb0("red2", [128, 1], F32)
        mm2_t = sb0("mm2_t", [128, 2], F32)
        cmm2_t = sb0("cmm2_t", [128, 2], F32)
        var_t = sb0("var_t", [128, 1], F32)
        rstd_t = sb0("rstd_t", [128, 1], F32)
        ln_sc = sb0("ln_sc", [128, 1024], F32)
        ln_bf = sb0("ln_bf", [128, 1024], BF16)
        si_sc = sb0("si_sc", [128, WSH], F32)
        tstage = sb0("tstage", [128, 1024], BF16)

        qbuf = {}

        def alloc_quant(stack, wmax, tag):
            sbq = lambda n, s, d: stack.enter_context(
                nc.sbuf_tensor(n + tag, s, d))
            rnb = wmax // 16
            qbuf["RNB"] = rnb
            qbuf["qs_u"] = sbq("qs_u", [128, wmax], F32)
            qbuf["qs_g"] = sbq("qs_g", [128, wmax], BF16)
            qbuf["qb1"] = sbq("qb1", [128, wmax], BF16)
            qbuf["qb2"] = sbq("qb2", [128, wmax], BF16)
            qbuf["qb3"] = sbq("qb3", [128, wmax], BF16)
            qbuf["m_sc"] = sbq("m_sc", [128, rnb], F32)
            qbuf["sc_sc"] = sbq("sc_sc", [128, 2 * rnb], F32)
            qbuf["csr_sc"] = sbq("csr_sc", [128, 2 * rnb], F32)
            qbuf["hq"] = sbq("hq", [128, wmax], BF16)

        ps_wide = st0.enter_context(nc.psum_tensor("ps_wide", [128, 2048], F32))
        ps_a = st0.enter_context(nc.psum_tensor("ps_a", [128, 512], F32))
        ps_b = st0.enter_context(nc.psum_tensor("ps_b", [128, 512], F32))
        ps_z = st0.enter_context(nc.psum_tensor("ps_z", [128, 512], F32))

        ps_tr = st0.enter_context(nc.psum_tensor("ps_tr", [128, 1024], BF16))

        names = ["ld", "ldg", "st", "cc", "sq", "var2", "rstd", "sgn", "t2d",
                 "q", "tp", "te", "gu", "si", "hmul", "dn", "x1", "wld",
                 "wld2", "vmm", "vev", "kmm", "kev", "qmm", "qev", "onm",
                 "zn", "nev", "nmr", "qld", "sT", "exp", "av", "oev",
                 "ost0", "ost1", "qst0", "qst1", "rld", "wo", "outd", "ph",
                 "scl", "sclc", "vv"]
        SM = {nm: Sem(st0, nc, nm) for nm in names}
        block = st0.enter_context(nc.Block())

        def V(f): block.vector(f)
        def Sc(f): block.scalar(f)
        def Tn(f): block.tensor(f)
        def Sy(f): block.sync(f)
        def G(f): block.gpsimd(f)

        PH = SM["ph"]
        phase_no = [0]

        def phase_barrier():
            phase_no[0] += 1
            tgt = 5 * phase_no[0]
            @V
            def _(v):
                v.sem_inc(PH.h, 1)
                v.wait_ge(PH.h, tgt)
            @Sc
            def _(s):
                s.sem_inc(PH.h, 1)
                s.wait_ge(PH.h, tgt)
            @Tn
            def _(t):
                t.sem_inc(PH.h, 1)
                t.wait_ge(PH.h, tgt)
            @Sy
            def _(s):
                s.sem_inc(PH.h, 1)
                s.wait_ge(PH.h, tgt)
            @G
            def _(g):
                g.sem_inc(PH.h, 1)
                g.wait_ge(PH.h, tgt)
            PH.n = tgt

        def emit_ln(src_ap, dst_ap, sc_waits=()):
            @V
            def _(v):
                SM["vv"].wait(v)
                for nm, val in sc_waits:
                    SM[nm].wait(v, val)
                i = v.reduce_sum(red1[:], src_ap, AX.X)
                SM["vv"].inc(i)
            @G
            def _(g):
                SM["q"].wait(g)
                for nm, val in sc_waits:
                    SM[nm].wait(g, val)
                i = g.tensor_mul(qbuf["qs_u"][:, 0:1024], src_ap, src_ap)
                SM["sq"].inc(i)
            @V
            def _(v):
                SM["vv"].wait(v)
                SM["sq"].wait(v)
                i = v.reduce_sum(red2[:], qbuf["qs_u"][:, 0:1024], AX.X)
                SM["vv"].inc(i)
                i = v.tensor_scalar_mul(mm2_t[:, 0:1], red1[:], 1.0 / D)
                SM["vv"].inc(i)
            @V
            def _(v):
                SM["vv"].wait(v)
                i = v.tensor_mul(mm2_t[:, 1:2], mm2_t[:, 0:1], mm2_t[:, 0:1])
                SM["vv"].inc(i)
            @V
            def _(v):
                SM["vv"].wait(v)
                i = v.scalar_tensor_tensor(var_t[:], red2[:], 1.0 / D,
                                           mm2_t[:, 1:2],
                                           ALU.mult, ALU.subtract)
                SM["var2"].inc(i)
            @Sc
            def _(s):
                SM["var2"].wait(s)
                i = s.activation(rstd_t[:], var_t[:], ACT.Sqrt,
                                 bias=eps_t[:, 0:1], scale=1.0)
                SM["rstd"].inc(i)
            @V
            def _(v):
                SM["vv"].wait(v)
                SM["rstd"].wait(v)
                i = v.reciprocal(mm2_t[:, 1:2], rstd_t[:])
                SM["vv"].inc(i)
            @G
            def _(g):
                SM["vv"].wait(g)
                i = g.tensor_copy(cmm2_t[:], mm2_t[:])
                SM["sclc"].inc(i)
            @V
            def _(v):
                SM["vv"].wait(v)
                SM["sclc"].wait(v)
                i = v.tensor_scalar(dst_ap, src_ap, cmm2_t[:, 0:1],
                                    cmm2_t[:, 1:2], ALU.subtract, ALU.mult)
                SM["q"].inc(i)

        def emit_quant(src_ap, out_ap, W, sc_waits=(), v_waits=()):
            nb = W // 16
            rnb = qbuf["RNB"]
            qs_u, qs_g = qbuf["qs_u"], qbuf["qs_g"]
            m_sc, sc_sc, csr_sc = qbuf["m_sc"], qbuf["sc_sc"], qbuf["csr_sc"]
            src3 = src_ap.rearrange("p (s n) -> p s n", n=16)
            @Sc
            def _(s):
                for nm, val in sc_waits:
                    SM[nm].wait(s, val)
                SM["q"].wait(s)
                i = s.sign(qs_g[:, 0:W], src_ap)
                SM["sgn"].inc(i)
                i = s.activation(qs_u[:, 0:W], src_ap, ACT.Abs)
                SM["sgn"].inc(i)
            u3 = qs_u[:, 0:W].rearrange("p (s n) -> p s n", n=16)
            u = qs_u[:, 0:W]
            b1, b2, b3 = (qbuf["qb1"][:, 0:W], qbuf["qb2"][:, 0:W],
                          qbuf["qb3"][:, 0:W])

            def vsec(*ops):
                waits, fns = [], []
                for o in ops:
                    (waits if isinstance(o, tuple) else fns).append(o)
                @V
                def _(v):
                    SM["vv"].wait(v)
                    for nm, val in waits:
                        SM[nm].wait(v, val)
                    for fn in fns:
                        i = fn(v)
                        SM["vv"].inc(i)

            vsec(*v_waits,
                 lambda v: v.tensor_reduce(m_sc[:, 0:nb], src3, AX.X, ALU.max,
                                           apply_absolute_value=True))
            vsec(lambda v: v.tensor_scalar(sc_sc[:, 0:nb], m_sc[:, 0:nb],
                                           1.0 / 6.0, 1e-12, ALU.mult,
                                           ALU.max))
            vsec(lambda v: v.reciprocal(sc_sc[:, rnb:rnb + nb], sc_sc[:, 0:nb]))
            @G
            def _(g):
                SM["vv"].wait(g)
                i = g.tensor_copy(csr_sc[:, 0:nb], sc_sc[:, 0:nb])
                SM["sclc"].inc(i)
                i = g.tensor_copy(csr_sc[:, rnb:rnb + nb],
                                  sc_sc[:, rnb:rnb + nb])
                SM["sclc"].inc(i)
            rb = csr_sc[:, rnb:rnb + nb].unsqueeze(2).broadcast_to(
                [128, nb, 16])
            scb = csr_sc[:, 0:nb].unsqueeze(2).broadcast_to([128, nb, 16])
            vsec(("sclc", None), ("tp", None), ("sgn", None),
                 lambda v: v.tensor_tensor(u3, u3, rb, ALU.mult))
            vsec(lambda v: v.tensor_scalar(b1, u, 0.25, 0.5, ALU.is_gt, ALU.mult),
                 lambda v: v.tensor_scalar(b2, u, 0.75, 0.5, ALU.is_gt, ALU.mult),
                 lambda v: v.tensor_scalar(b3, u, 1.25, 0.5, ALU.is_gt, ALU.mult),
                 lambda v: v.tensor_scalar(out_ap, u, 1.75, 0.5, ALU.is_gt, ALU.mult))
            vsec(lambda v: v.tensor_add(b1, b1, b2),
                 lambda v: v.tensor_add(b3, b3, out_ap))
            vsec(lambda v: v.tensor_scalar(b2, u, 2.5, 1.0, ALU.is_gt, ALU.mult),
                 lambda v: v.tensor_scalar(out_ap, u, 3.5, 1.0, ALU.is_gt, ALU.mult))
            vsec(lambda v: v.tensor_add(b1, b1, b3),
                 lambda v: v.tensor_add(b2, b2, out_ap))
            vsec(lambda v: v.tensor_scalar(b3, u, 5.0, 2.0, ALU.is_gt, ALU.mult),
                 lambda v: v.tensor_add(b1, b1, b2))
            vsec(lambda v: v.tensor_add(b1, b1, b3))
            vsec(lambda v: v.tensor_mul(u, b1, qs_g[:, 0:W]))
            @V
            def _(v):
                SM["vv"].wait(v)
                SM["sclc"].wait(v)
                i = v.tensor_tensor(
                    out_ap.rearrange("p (s n) -> p s n", n=16),
                    u3, scb, ALU.mult)
                SM["q"].inc(i)

        tp_slot = [0]

        def emit_transpose(src_ap, col0, nchunk, dst_ap, tn_waits=()):
            slot = tp_slot[0] % 2
            tp_slot[0] += 1
            my_te = SM["te"].n + 1
            @Tn
            def _(t):
                if tp_slot[0] > 2:
                    SM["te"].wait(t, my_te - 2)
                SM["q"].wait(t)
                for nm, val in tn_waits:
                    SM[nm].wait(t, val)
                i = None
                for c in range(nchunk):
                    i = t.transpose(
                        ps_tr[:, slot * 512 + c * 128:slot * 512 + (c + 1) * 128],
                        src_ap[:, col0 + c * 128:col0 + (c + 1) * 128],
                        ident[:])
                SM["tp"].inc(i)
            @Sc
            def _(s):
                SM["tp"].wait(s)
                i = s.copy(dst_ap,
                           ps_tr[:, slot * 512:slot * 512 + nchunk * 128]
                           .rearrange("p (c x) -> p c x", c=nchunk))
                SM["te"].inc(i)

        @Sy
        def _(sy):
            SM["ld"].dma(sy.dma_start(out=ident[:], in_=id_in))
            SM["ld"].dma(sy.dma_start(out=bq_sb[:], in_=bq_in))
            SM["ld"].dma(sy.dma_start(out=bk_sb[:], in_=bk_in))
        @G
        def _(g):
            g.memset(ones_r[:], 1.0)
            g.memset(ones_c[:], 1.0)
            g.memset(ones_q[:], 1.0)
            SM["ldg"].dma(g.dma_start(out=bv_sb[:], in_=bv_in))
            SM["ldg"].dma(g.dma_start(out=bo_sb[:], in_=bo_in))
        @V
        def _(v):
            v.memset(q2m[:], 0.0)
            v.memset(k2m[:], 0.0)
            v.memset(eps_t[:], 1e-5)

        stW = ExitStack()
        alloc_quant(stW, 1024, "W")
        wsh_g = stW.enter_context(nc.sbuf_tensor("wsh_g", [128, 3 * 1024], F32))
        wsh_u = stW.enter_context(nc.sbuf_tensor("wsh_u", [128, 3 * 1024], F32))
        wsh_d = stW.enter_context(nc.sbuf_tensor("wsh_d", [128, 8 * WSH], F32))
        wq_bf = stW.enter_context(nc.sbuf_tensor("wq_bf", [128, 1024], BF16))
        @Sy
        def _(sy):
            SM["ld"].dma(sy.dma_start(
                out=wsh_g[:].rearrange("p (t d) -> p t d", t=3),
                in_=gsh_in.rearrange("(t p) d -> p t d", p=128)))
            SM["ld"].dma(sy.dma_start(
                out=wsh_u[:].rearrange("p (t d) -> p t d", t=3),
                in_=ush_in.rearrange("(t p) d -> p t d", p=128)))
            SM["ld"].dma(sy.dma_start(
                out=wsh_d[:].rearrange("p (t w) -> p t w", t=8),
                in_=dsh_in.rearrange("(t p) w -> p t w", p=128)))
        @V
        def _(v):
            SM["ld"].wait(v)
        @Sc
        def _(s):
            SM["ld"].wait(s)
            SM["ldg"].wait(s)
        @Tn
        def _(t):
            SM["ld"].wait(t)
            SM["ldg"].wait(t)

        def w_qts(src_sb, tile_i, W, dst_dram_ap):
            nchunk = W // 128
            src = src_sb[:, tile_i * W:(tile_i + 1) * W]
            emit_quant(src, wq_bf[:, 0:W], W)
            for h in range(0, nchunk, 4):
                nch = min(4, nchunk - h)
                emit_transpose(
                    wq_bf[:, 0:W], h * 128, nch,
                    tstage[:, 0:nch * 128].rearrange("p (c x) -> p c x", c=nch))
                @Sy
                def _(sy, h=h, nch=nch):
                    SM["te"].wait(sy)
                    SM["st"].dma(sy.dma_start(
                        out=dst_dram_ap[h:h + nch].transpose([1, 0, 2]),
                        in_=tstage[:, 0:nch * 128]
                        .rearrange("p (c x) -> p c x", c=nch)))
                @Sc
                def _(s):
                    SM["st"].wait(s)

        for src_sb, bounce in ((wsh_g, wg_in), (wsh_u, wu_in)):
            for t3 in range(3):
                dst = bounce.ap().rearrange(
                    "(c p) (t x) -> c p t x", p=128, x=128)[:, :, t3, :]
                w_qts(src_sb, t3, 1024, dst)
        for t8 in range(8):
            dst = wd_in.ap().rearrange(
                "(c p) (t x) -> c p t x", p=128, x=128)[:, :, t8, :]
            w_qts(wsh_d, t8, WSH, dst)

        @G
        def _(g):
            SM["st"].wait(g)
            for a, b in ((wg_in, wg_full), (wu_in, wu_full), (wd_in, wd_full)):
                i = g.collective_compute(
                    "AllGather", ALU.bypass, replica_groups=GROUP8,
                    ins=[a.ap().opt()], outs=[b.ap().opt()])
                SM["cc"].inc(i)
        CC_W = SM["cc"].n

        stW.close()
        phase_barrier()

        wstate = {"wfree": [], "wdfree": []}

        def emit_ffn(x_sb, ntile, base_tok, bufs, ln2_spill, out_write,
                     x_ready):
            STW = ntile * 128
            h_full = bufs["h"]
            xqT = bufs["xqT"][:].rearrange("p (c t) -> p c t", c=8)
            hqT = bufs["hqT"][:].rearrange("p (c t) -> p c t", c=NIC)
            wg3 = bufs["wg"][:].rearrange("p (b c w) -> p b c w", b=2, c=8)
            wu3 = bufs["wu"][:].rearrange("p (b c w) -> p b c w", b=2, c=8)
            wd3 = bufs["wd"][:].rearrange("p (c d) -> p c d", c=NIC)

            for tl in range(ntile):
                xt = x_sb[:, tl * 1024:(tl + 1) * 1024]
                emit_ln(xt, ln_sc[:], sc_waits=x_ready)
                emit_quant(ln_sc[:], ln_bf[:], 1024,
                           v_waits=(("q", SM["q"].n),))
                for h in (0, 4):
                    emit_transpose(ln_bf[:], h * 128, 4,
                                   xqT[:, h:h + 4, tl * 128:(tl + 1) * 128])

            @Sy
            def _(sy):
                sy.wait_ge(SM["cc"].h, CC_W)
                if wstate["wdfree"]:
                    SM["dn"].wait(sy, wstate["wdfree"][-1])
                SM["wld2"].dma(sy.dma_start(
                    out=wd3,
                    in_=wd_full.ap().rearrange(
                        "(c p) d -> p c d", p=128)[:, 0:NIC, :]))

            hm_last = [0] * ntile
            for r, (off, w) in enumerate(FFN_BLOCKS):
                par = len(wstate["wfree"]) % 2
                @Sy
                def _(sy, r=r, par=par, w=w):
                    sy.wait_ge(SM["cc"].h, CC_W)
                    hist = wstate["wfree"]
                    if len(hist) >= 2:
                        SM["gu"].wait(sy, hist[-2])
                    SM["wld"].dma(sy.dma_start(
                        out=wg3[:, par, :, 0:w],
                        in_=wg_full.ap().rearrange(
                            "(r c p) w -> r p c w", c=8, p=128)[r][:, :, 0:w]))
                    SM["wld"].dma(sy.dma_start(
                        out=wu3[:, par, :, 0:w],
                        in_=wu_full.ap().rearrange(
                            "(r c p) w -> r p c w", c=8, p=128)[r][:, :, 0:w]))
                for tl in range(ntile):
                    @Tn
                    def _(t, par=par, w=w, tl=tl):
                        SM["wld"].wait(t)
                        SM["te"].wait(t)
                        SM["si"].wait(t)
                        SM["hmul"].wait(t)
                        i = None
                        for fc in range(8):
                            i = t.matmul(ps_a[:, 0:w],
                                         xqT[:, fc, tl * 128:(tl + 1) * 128],
                                         wg3[:, par, fc, 0:w],
                                         start=(fc == 0), stop=(fc == 7))
                        SM["gu"].inc(i)
                        for fc in range(8):
                            i = t.matmul(ps_b[:, 0:w],
                                         xqT[:, fc, tl * 128:(tl + 1) * 128],
                                         wu3[:, par, fc, 0:w],
                                         start=(fc == 0), stop=(fc == 7))
                        SM["gu"].inc(i)
                    @Sc
                    def _(s, w=w):
                        SM["gu"].wait(s)
                        i = s.activation(si_sc[:, 0:w], ps_a[:, 0:w], ACT.Silu)
                        SM["si"].inc(i)
                    @V
                    def _(v, w=w, off=off, tl=tl):
                        SM["si"].wait(v)
                        i = v.tensor_tensor(
                            h_full[:, tl * IFF + off:tl * IFF + off + w],
                            si_sc[:, 0:w], ps_b[:, 0:w], ALU.mult)
                        SM["hmul"].inc(i)
                    hm_last[tl] = SM["hmul"].n
                wstate["wfree"].append(SM["gu"].n)

            for tl in range(ntile):
                emit_quant(h_full[:, tl * IFF:(tl + 1) * IFF],
                           qbuf["hq"][:, 0:IFF], IFF,
                           sc_waits=(("hmul", hm_last[tl]),),
                           v_waits=(("hmul", hm_last[tl]),))
                for hh in range(0, NIC, 4):
                    nch = min(4, NIC - hh)
                    emit_transpose(qbuf["hq"][:, 0:IFF], hh * 128, nch,
                                   hqT[:, hh:hh + nch,
                                       tl * 128:(tl + 1) * 128])

            for tl in range(ntile):
                @Tn
                def _(t, tl=tl):
                    SM["wld2"].wait(t)
                    SM["te"].wait(t)
                    SM["x1"].wait(t)
                    i = None
                    for ic in range(NIC):
                        stat = hqT[:, ic, tl * 128:(tl + 1) * 128]
                        i = t.matmul(ps_wide[:, 0:512], stat, wd3[:, ic, 0:512],
                                     start=(ic == 0), stop=(ic == NIC - 1))
                        i = t.matmul(ps_wide[:, 512:1024], stat,
                                     wd3[:, ic, 512:1024],
                                     start=(ic == 0), stop=(ic == NIC - 1))
                    SM["dn"].inc(i)
                x1_rec = SM["x1"].n + 1
                @V
                def _(v, tl=tl):
                    SM["x1"].wait(v)
                    SM["dn"].wait(v)
                    xt = x_sb[:, tl * 1024:(tl + 1) * 1024]
                    i = v.scalar_tensor_tensor(xt, ps_wide[:, 0:1024], 0.5,
                                               xt, ALU.mult, ALU.add)
                    SM["x1"].inc(i)
                if ln2_spill:
                    xt = x_sb[:, tl * 1024:(tl + 1) * 1024]
                    tok = base_tok + tl * 128
                    emit_ln(xt, ln_bf[:], sc_waits=(("x1", x1_rec),))
                    for h in (0, 4):
                        emit_transpose(
                            ln_bf[:], h * 128, 4,
                            tstage[:, 0:512].rearrange("p (c x) -> p c x", c=4),
                            tn_waits=(("st", SM["st"].n),))
                        @Sy
                        def _(sy, h=h, tok=tok):
                            SM["te"].wait(sy)
                            SM["st"].dma(sy.dma_start(
                                out=ln2_sp.ap()[h:h + 4, :, tok:tok + 128]
                                .transpose([1, 0, 2]),
                                in_=tstage[:, 0:512]
                                .rearrange("p (c x) -> p c x", c=4)))
                    @Sy
                    def _(sy, tl=tl, tok=tok):
                        SM["x1"].wait(sy)
                        SM["st"].dma(sy.dma_start(
                            out=x1_sp.ap()[tok:tok + 128, :],
                            in_=x_sb[:, tl * 1024:(tl + 1) * 1024]))
                if out_write is not None:
                    out_write(tl)
            wstate["wdfree"].append(SM["dn"].n)

        stF1 = ExitStack()
        sbF = lambda n, s, d: stF1.enter_context(nc.sbuf_tensor(n, s, d))
        alloc_quant(stF1, IFF, "F")
        x_st = sbF("x_st", [128, TPS * 1024], F32)
        f1bufs = {
            "xqT": sbF("xqT1", [128, 8 * SUP], BF16),
            "hqT": sbF("hqT1", [128, NIC * SUP], BF16),
            "h": sbF("h1", [128, TPS * IFF], BF16),
            "wg": sbF("wg1", [128, 2 * 8 * WSH], BF16),
            "wu": sbF("wu1", [128, 2 * 8 * WSH], BF16),
            "wd": sbF("wd1", [128, NIC * 1024], BF16),
        }
        for sti in range(NSUP):
            @Sy
            def _(sy, sti=sti):
                if sti > 0:
                    SM["x1"].wait(sy)
                    SM["sq"].wait(sy)
                    SM["st"].wait(sy)
                SM["ld"].dma(sy.dma_start(
                    out=x_st[:].rearrange("p (t d) -> p t d", t=TPS),
                    in_=x_in.rearrange("(t p) d -> p t d", p=128)
                    [:, sti * TPS:(sti + 1) * TPS, :]))
            ld_rec = SM["ld"].n
            emit_ffn(x_st[:], TPS, sti * SUP, f1bufs, ln2_spill=True,
                     out_write=None, x_ready=(("ld", ld_rec),))

        stF1.close()
        phase_barrier()

        stP = ExitStack()
        sbP = lambda n, s, d: stP.enter_context(nc.sbuf_tensor(n, s, d))
        K_sb = sbP("K_sb", [128, 16 * T], BF16)
        V_sb = sbP("V_sb", [128, KJ * 2048], BF16)
        K3 = K_sb[:].rearrange("p (c k) -> p c k", c=16)
        V3 = V_sb[:].rearrange("p (j d) -> p j d", j=KJ)
        stP3 = ExitStack()
        sbP3 = lambda n, s, d: stP3.enter_context(nc.sbuf_tensor(n, s, d))
        aw_sb = sbP3("aw_sb", [128, 8 * 1024], BF16)
        aw3 = aw_sb[:].rearrange("p (c d) -> p c d", c=8)
        l2b = sbP3("l2b", [128, 8 * 512], BF16)
        l2b3 = l2b[:].rearrange("p (c t) -> p c t", c=8)
        qstage = sbP3("qstage", [128, 2 * 512], BF16)
        sqst = sbP3("sqst", [128, 512], BF16)
        q2part = sbP3("q2part", [1, T], F32)
        k2part = sbP3("k2part", [1, T], F32)

        def load_ln2(tb, waits):
            @Sy
            def _(sy, tb=tb):
                for nm, val in waits:
                    SM[nm].wait(sy, val)
                SM["ld"].dma(sy.dma_start(
                    out=l2b3,
                    in_=ln2_sp.ap()[:, :, tb * 512:(tb + 1) * 512]
                    .transpose([1, 0, 2])))

        def load_awhalf(src_dram, half, waits):
            @G
            def _(g, half=half):
                for nm, val in waits:
                    SM[nm].wait(g, val)
                SM["ldg"].dma(g.dma_start(
                    out=aw3,
                    in_=src_dram.rearrange("(c p) e -> p c e", p=128)
                    [:, :, half * 1024:(half + 1) * 1024]))

        qcnt = [0]
        for half in range(2):
            load_awhalf(wqT_d, half,
                        [("qmm", SM["qmm"].n)] if half else [])
            for tb in range(NTB):
                load_ln2(tb, [("qmm", SM["qmm"].n)] if (half or tb) else [])
                for dch in range(8):
                    dc = half * 8 + dch
                    pp = ps_a if (dch % 2 == 0) else ps_b
                    par = qcnt[0] % 2
                    qcnt[0] += 1
                    @Tn
                    def _(t, dch=dch, pp=pp):
                        SM["ld"].wait(t)
                        SM["ldg"].wait(t)
                        SM["qev"].wait(t, max(0, SM["qev"].n - 1))
                        i = None
                        for fc in range(8):
                            i = t.matmul(pp[:, 0:512],
                                         aw3[:, fc, dch * 128:dch * 128 + 128],
                                         l2b3[:, fc, :],
                                         start=(fc == 0), stop=(fc == 7))
                        SM["qmm"].inc(i)
                    @Sc
                    def _(s, dc=dc, pp=pp, par=par):
                        SM["qmm"].wait(s)
                        SM[f"qst{par}"].wait(s)
                        i = s.activation(qstage[:, par * 512:(par + 1) * 512],
                                         pp[:, 0:512], ACT.Identity,
                                         bias=bq_sb[:, dc:dc + 1], scale=1.0)
                        SM["qev"].inc(i)
                    @V
                    def _(v, par=par):
                        SM["qev"].wait(v)
                        i = v.tensor_mul(sqst[:],
                                         qstage[:, par * 512:(par + 1) * 512],
                                         qstage[:, par * 512:(par + 1) * 512])
                        SM["onm"].inc(i)
                    @Tn
                    def _(t, dch=dch):
                        SM["onm"].wait(t)
                        if dch == 0:
                            SM["nev"].wait(t)
                        i = t.matmul(ps_z[0:1, 0:512], ones_c[:], sqst[:],
                                     start=(dch == 0), stop=(dch == 7))
                        if dch == 7:
                            SM["zn"].inc(i)
                    @Sy
                    def _(sy, tb=tb, dc=dc, par=par):
                        SM["qev"].wait(sy)
                        for s2 in range(512 // QB):
                            qb = (tb * 512) // QB + s2
                            SM[f"qst{par}"].dma(sy.dma_start(
                                out=qt_in.ap()[qb * 2048 + dc * 128:
                                               qb * 2048 + dc * 128 + 128, :],
                                in_=qstage[:, par * 512 + s2 * QB:
                                           par * 512 + (s2 + 1) * QB]))
                if half == 0:
                    @V
                    def _(v, tb=tb):
                        SM["nev"].wait(v)
                        SM["zn"].wait(v)
                        i = v.tensor_copy(q2part[0:1, tb * 512:(tb + 1) * 512],
                                          ps_z[0:1, 0:512])
                        SM["nev"].inc(i)
                else:
                    @V
                    def _(v, tb=tb):
                        SM["nev"].wait(v)
                        SM["zn"].wait(v)
                        i = v.tensor_add(q2part[0:1, tb * 512:(tb + 1) * 512],
                                         q2part[0:1, tb * 512:(tb + 1) * 512],
                                         ps_z[0:1, 0:512])
                        SM["nev"].inc(i)
                    @V
                    def _(v, tb=tb):
                        SM["nev"].wait(v)
                        i = v.reduce_max(t11[:],
                                         q2part[0:1, tb * 512:(tb + 1) * 512],
                                         AX.X)
                        SM["nev"].inc(i)
                    @V
                    def _(v):
                        SM["nev"].wait(v)
                        i = v.tensor_max(q2m[:], q2m[:], t11[:])
                        SM["nev"].inc(i)

        @G
        def _(g):
            SM["qst0"].wait(g)
            SM["qst1"].wait(g)
            i = g.collective_compute(
                "AllGather", ALU.bypass, replica_groups=GROUPS4,
                ins=[qt_in.ap().opt()], outs=[qt_full.ap().opt()])
            SM["cc"].inc(i)
        CC_QT = SM["cc"].n

        for half in range(2):
            load_awhalf(wkT_d, half, [("kmm", SM["kmm"].n)] if half
                        else [("qmm", SM["qmm"].n)])
            for kb in range(NTB):
                load_ln2(kb, [("kmm", SM["kmm"].n)] if (half or kb)
                         else [("qmm", SM["qmm"].n)])
                for dch in range(8):
                    dc = half * 8 + dch
                    pp = ps_a if (dch % 2 == 0) else ps_b
                    @Tn
                    def _(t, dch=dch, pp=pp):
                        SM["ld"].wait(t)
                        SM["ldg"].wait(t)
                        SM["kev"].wait(t, max(0, SM["kev"].n - 1))
                        i = None
                        for fc in range(8):
                            i = t.matmul(pp[:, 0:512],
                                         aw3[:, fc, dch * 128:dch * 128 + 128],
                                         l2b3[:, fc, :],
                                         start=(fc == 0), stop=(fc == 7))
                        SM["kmm"].inc(i)
                    @Sc
                    def _(s, dc=dc, kb=kb, pp=pp):
                        SM["kmm"].wait(s)
                        i = s.activation(K3[:, dc, kb * 512:(kb + 1) * 512],
                                         pp[:, 0:512], ACT.Identity,
                                         bias=bk_sb[:, dc:dc + 1], scale=1.0)
                        SM["kev"].inc(i)
                    @V
                    def _(v, dc=dc, kb=kb):
                        SM["kev"].wait(v)
                        i = v.tensor_mul(sqst[:],
                                         K3[:, dc, kb * 512:(kb + 1) * 512],
                                         K3[:, dc, kb * 512:(kb + 1) * 512])
                        SM["onm"].inc(i)
                    @Tn
                    def _(t, dch=dch):
                        SM["onm"].wait(t)
                        if dch == 0:
                            SM["nev"].wait(t)
                        i = t.matmul(ps_z[0:1, 0:512], ones_c[:], sqst[:],
                                     start=(dch == 0), stop=(dch == 7))
                        if dch == 7:
                            SM["zn"].inc(i)
                if half == 0:
                    @V
                    def _(v, kb=kb):
                        SM["nev"].wait(v)
                        SM["zn"].wait(v)
                        i = v.tensor_copy(k2part[0:1, kb * 512:(kb + 1) * 512],
                                          ps_z[0:1, 0:512])
                        SM["nev"].inc(i)
                else:
                    @V
                    def _(v, kb=kb):
                        SM["nev"].wait(v)
                        SM["zn"].wait(v)
                        i = v.tensor_add(k2part[0:1, kb * 512:(kb + 1) * 512],
                                         k2part[0:1, kb * 512:(kb + 1) * 512],
                                         ps_z[0:1, 0:512])
                        SM["nev"].inc(i)
                    @V
                    def _(v, kb=kb):
                        SM["nev"].wait(v)
                        i = v.reduce_max(t11[:],
                                         k2part[0:1, kb * 512:(kb + 1) * 512],
                                         AX.X)
                        SM["nev"].inc(i)
                    @V
                    def _(v):
                        SM["nev"].wait(v)
                        i = v.tensor_max(k2m[:], k2m[:], t11[:])
                        SM["nev"].inc(i)

        @V
        def _(v):
            SM["nev"].wait(v)
            i = v.tensor_copy(nrm_pk[0:1, 0:1], q2m[:])
            SM["onm"].inc(i)
            i = v.tensor_copy(nrm_pk[0:1, 1:2], k2m[:])
            SM["onm"].inc(i)
        @Sy
        def _(sy):
            SM["onm"].wait(sy)
            SM["st"].dma(sy.dma_start(out=nrm_in.ap(), in_=nrm_pk[:]))
        @G
        def _(g):
            SM["st"].wait(g)
            i = g.collective_compute(
                "AllGather", ALU.bypass, replica_groups=GROUPS4,
                ins=[nrm_in.ap().opt()], outs=[nrm_out.ap().opt()])
            SM["cc"].inc(i)
        CC_NRM = SM["cc"].n
        @Sy
        def _(sy):
            sy.wait_ge(SM["cc"].h, CC_NRM)
            SM["ld"].dma(sy.dma_start(
                out=nrm_sb[:], in_=nrm_out.ap().rearrange("a b -> (a b)").unsqueeze(0)))
        @V
        def _(v):
            SM["vv"].wait(v)
            SM["ld"].wait(v)
            i = v.tensor_reduce(nrm_pk[0:1, 0:2],
                                nrm_sb[:].rearrange("p (a b) -> p b a", b=2),
                                AX.X, ALU.max)
            SM["vv"].inc(i)
        @V
        def _(v):
            SM["vv"].wait(v)
            i = v.tensor_mul(t11[:], nrm_pk[0:1, 0:1], nrm_pk[0:1, 1:2])
            SM["onm"].inc(i)
        @Sc
        def _(s):
            SM["onm"].wait(s)
            i = s.activation(rstd_t[0:1, 0:1], t11[:], ACT.Sqrt)
            SM["rstd"].inc(i)
        @V
        def _(v):
            SM["rstd"].wait(v)
            i = v.tensor_scalar_mul(t11b[:], rstd_t[0:1, 0:1], -1.0)
            SM["onm"].inc(i)
        @Tn
        def _(t):
            SM["onm"].wait(t)
            SM["nev"].wait(t)
            i = t.matmul(ps_z[0:1, 0:128], t11b[:], ones_r[:],
                         start=True, stop=True)
            SM["zn"].inc(i)
        @V
        def _(v):
            SM["zn"].wait(v)
            i = v.tensor_copy(mrow_b[:], ps_z[0:1, 0:128])
            SM["nmr"].inc(i)
        NMR = SM["nmr"].n

        for half in range(2):
            load_awhalf(wvT_d, half, [("vmm", SM["vmm"].n)] if half
                        else [("kmm", SM["kmm"].n)])
            for tb in range(NTB):
                load_ln2(tb, [("vmm", SM["vmm"].n)] if (half or tb)
                         else [("kmm", SM["kmm"].n)])
                for tw in range(4):
                    tl = tb * 4 + tw
                    @Tn
                    def _(t, half=half, tw=tw):
                        SM["ld"].wait(t)
                        SM["ldg"].wait(t)
                        SM["vev"].wait(t)
                        i = None
                        for n in range(2):
                            nn = half * 1024 + n * 512
                            for fc in range(8):
                                i = t.matmul(
                                    ps_wide[:, n * 512:(n + 1) * 512],
                                    l2b3[:, fc, tw * 128:(tw + 1) * 128],
                                    aw3[:, fc, n * 512:(n + 1) * 512],
                                    start=(fc == 0), stop=False)
                            i = t.matmul(ps_wide[:, n * 512:(n + 1) * 512],
                                         ones_r[:], bv_sb[:, nn:nn + 512],
                                         start=False, stop=True)
                        SM["vmm"].inc(i)
                    @Sc
                    def _(s, half=half, tl=tl):
                        SM["vmm"].wait(s)
                        i = s.copy(V3[:, tl, half * 1024:(half + 1) * 1024],
                                   ps_wide[:, 0:1024])
                        SM["vev"].inc(i)

        phase_barrier()

        stP3.close()
        qtb = sbP("qtb", [128, 2 * 16 * QB], BF16)
        qtb4 = qtb[:].rearrange("p (b c q) -> p b c q", b=2, c=16)
        ptb = sbP("ptb", [128, KJ * QB], BF16)
        ptb3 = ptb[:].rearrange("p (j q) -> p j q", j=KJ)
        ostage = sbP("ostage", [128, 2 * 2052], F32)

        @V
        def _(v):
            v.memset(ostage[:, 2049:2052], 0.0)
            v.memset(ostage[:, 2052 + 2049:2052 + 2052], 0.0)

        qfree = [0, 0]
        exp_par = [0, 0]
        rs_cc = []
        qn = [0]
        on = [0]
        for g in range(4):
            for o in range(4):
                for qs2 in range(NQS):
                    qb_own = g * NQS + qs2
                    qb_glob = o * NQB_OWN + qb_own
                    par = qn[0] % 2
                    qn[0] += 1
                    @Sy
                    def _(sy, qb_glob=qb_glob, par=par):
                        sy.wait_ge(SM["cc"].h, CC_QT)
                        SM["sT"].wait(sy, qfree[par])
                        SM["qld"].dma(sy.dma_start(
                            out=qtb4[:, par],
                            in_=qt_full.ap().rearrange(
                                "(b c p) q -> b p c q", c=16, p=128)[qb_glob]))
                    for j in range(KJ):
                        pp = ps_a if (j % 2 == 0) else ps_b
                        ppi = j % 2
                        @Tn
                        def _(t, j=j, pp=pp, ppi=ppi, par=par):
                            SM["qld"].wait(t)
                            t.wait_ge(SM["nmr"].h, NMR)
                            SM["exp"].wait(t, exp_par[ppi])
                            for dc in range(16):
                                t.matmul(pp[:, 0:QB],
                                         K3[:, dc, j * 128:(j + 1) * 128],
                                         qtb4[:, par, dc, :],
                                         start=(dc == 0), stop=False)
                            i = t.matmul(pp[:, 0:QB], mrow_b[:, j * 0:128],
                                         ones_q[:, 0:QB],
                                         start=False, stop=True)
                            SM["sT"].inc(i)
                        @Sc
                        def _(s, j=j, pp=pp):
                            SM["sT"].wait(s)
                            i = s.activation(ptb3[:, j, :], pp[:, 0:QB],
                                             ACT.Exp, bias=0.0,
                                             scale=1.0 / 32.0)
                            SM["exp"].inc(i)
                        exp_par[ppi] = SM["exp"].n
                    for ss in range(NSS):
                        opar = on[0] % 2
                        on[0] += 1
                        @Tn
                        def _(t, ss=ss):
                            SM["exp"].wait(t)
                            SM["oev"].wait(t)
                            i = None
                            for j in range(KJ):
                                stat = ptb3[:, j, ss * 128:(ss + 1) * 128]
                                for n in range(4):
                                    t.matmul(ps_wide[:, n * 512:(n + 1) * 512],
                                             stat, V3[:, j, n * 512:(n + 1) * 512],
                                             start=(j == 0), stop=(j == KJ - 1))
                                i = t.matmul(ps_z[:, ss:ss + 1], stat, ones_c[:],
                                             start=(j == 0), stop=(j == KJ - 1))
                            SM["av"].inc(i)
                        @V
                        def _(v, ss=ss, opar=opar):
                            SM["av"].wait(v)
                            SM[f"ost{opar}"].wait(v)
                            i = v.tensor_copy(
                                ostage[:, opar * 2052:opar * 2052 + 2048],
                                ps_wide[:])
                            SM["oev"].inc(i)
                            i = v.tensor_copy(
                                ostage[:, opar * 2052 + 2048:opar * 2052 + 2049],
                                ps_z[:, ss:ss + 1])
                            SM["oev"].inc(i)
                        @Sy
                        def _(sy, g=g, o=o, qs2=qs2, ss=ss, opar=opar):
                            SM["oev"].wait(sy)
                            row0 = o * SUB + qs2 * QB + ss * 128
                            SM[f"ost{opar}"].dma(sy.dma_start(
                                out=opart[g].ap()[row0:row0 + 128, :],
                                in_=ostage[:, opar * 2052:(opar + 1) * 2052]))
                    qfree[par] = SM["sT"].n
            @G
            def _(g_, g=g):
                SM["ost0"].wait(g_)
                SM["ost1"].wait(g_)
                i = g_.collective_compute(
                    "ReduceScatter", ALU.add, replica_groups=GROUPS4,
                    ins=[opart[g].ap().opt()], outs=[ors[g].ap().opt()])
                SM["cc"].inc(i)
            rs_cc.append(SM["cc"].n)

        stP.close()
        phase_barrier()

        stP5 = ExitStack()
        sbQ = lambda n, s, d: stP5.enter_context(nc.sbuf_tensor(n, s, d))
        woT_sb = sbQ("woT_sb", [128, 16 * 1024], BF16)
        woT3 = woT_sb[:].rearrange("p (c d) -> p c d", c=16)
        rsb = sbQ("rsb", [128, 2052], F32)
        x_g = sbQ("x_g", [128, TPG * 1024], F32)
        obf = sbQ("obf", [128, 2048], BF16)
        oT = sbQ("oT", [128, 16 * 128], BF16)
        oT3 = oT[:].rearrange("p (c x) -> p c x", c=16)
        rz_t = sbQ("rz_t", [128, 1], F32)
        @G
        def _(g):
            SM["ldg"].dma(g.dma_start(
                out=woT3, in_=woT_d.rearrange("(c p) d -> p c d", p=128)))
        @Sy
        def _(sy):
            SM["st"].wait(sy)

        for g in range(4):
            for tl in range(TPG):
                tok = g * SUB + tl * 128
                @Sy
                def _(sy, g=g, tl=tl, tok=tok):
                    sy.wait_ge(SM["cc"].h, rs_cc[g])
                    SM["q"].wait(sy)
                    SM["st"].wait(sy)
                    SM["rld"].dma(sy.dma_start(
                        out=rsb[:], in_=ors[g].ap()[tl * 128:(tl + 1) * 128, :]))
                    SM["rld"].dma(sy.dma_start(
                        out=x_g[:, tl * 1024:(tl + 1) * 1024],
                        in_=x1_sp.ap()[tok:tok + 128, :]))
                @V
                def _(v):
                    SM["rld"].wait(v)
                    i = v.reciprocal(rz_t[:], rsb[:, 2048:2049])
                    SM["scl"].inc(i)
                @Sc
                def _(s):
                    SM["scl"].wait(s)
                    i = s.mul(obf[:], rsb[:, 0:2048], rz_t[:, 0:1])
                    SM["q"].inc(i)
                for h in (0, 4, 8, 12):
                    emit_transpose(obf[:], h * 128, 4, oT3[:, h:h + 4, :])
                @Tn
                def _(t, tl=tl):
                    SM["te"].wait(t)
                    SM["ldg"].wait(t)
                    SM["x1"].wait(t)
                    i = None
                    for n in range(2):
                        for dc in range(16):
                            i = t.matmul(ps_wide[:, n * 512:(n + 1) * 512],
                                         oT3[:, dc, :],
                                         woT3[:, dc, n * 512:(n + 1) * 512],
                                         start=(dc == 0), stop=False)
                        i = t.matmul(ps_wide[:, n * 512:(n + 1) * 512],
                                     ones_r[:], bo_sb[:, n * 512:(n + 1) * 512],
                                     start=False, stop=True)
                    SM["wo"].inc(i)
                @V
                def _(v, tl=tl):
                    SM["wo"].wait(v)
                    SM["rld"].wait(v)
                    xt = x_g[:, tl * 1024:(tl + 1) * 1024]
                    i = v.tensor_tensor(xt, ps_wide[:, 0:1024], xt, ALU.add)
                    SM["x1"].inc(i)
                @Sy
                def _(sy, tl=tl, tok=tok):
                    SM["x1"].wait(sy)
                    SM["st"].dma(sy.dma_start(
                        out=x2_sp.ap()[tok:tok + 128, :],
                        in_=x_g[:, tl * 1024:(tl + 1) * 1024]))

        stP5.close()
        phase_barrier()

        stP6 = ExitStack()
        sbG = lambda n, s, d: stP6.enter_context(nc.sbuf_tensor(n, s, d))
        alloc_quant(stP6, IFF, "G")
        x_st2 = sbG("x_st2", [128, TPS * 1024], F32)
        f2bufs = {
            "xqT": sbG("xqT2", [128, 8 * SUP], BF16),
            "hqT": sbG("hqT2", [128, NIC * SUP], BF16),
            "h": sbG("h2", [128, TPS * IFF], BF16),
            "wg": sbG("wg2", [128, 2 * 8 * WSH], BF16),
            "wu": sbG("wu2", [128, 2 * 8 * WSH], BF16),
            "wd": sbG("wd2", [128, NIC * 1024], BF16),
        }
        for sti in range(NSUP):
            @Sy
            def _(sy, sti=sti):
                SM["x1"].wait(sy)
                SM["sq"].wait(sy)
                SM["st"].wait(sy)
                SM["outd"].wait(sy)
                SM["ld"].dma(sy.dma_start(
                    out=x_st2[:].rearrange("p (t d) -> p t d", t=TPS),
                    in_=x2_sp.ap().rearrange("(t p) d -> p t d", p=128)
                    [:, sti * TPS:(sti + 1) * TPS, :]))
            ld_rec = SM["ld"].n

            def out_write(tl, sti=sti):
                @Sy
                def _(sy, tl=tl, sti=sti):
                    SM["x1"].wait(sy)
                    tok = sti * SUP + tl * 128
                    SM["outd"].dma(sy.dma_start(
                        out=out_d[tok:tok + 128, :],
                        in_=x_st2[:, tl * 1024:(tl + 1) * 1024]))

            emit_ffn(x_st2[:], TPS, sti * SUP, f2bufs, ln2_spill=False,
                     out_write=out_write, x_ready=(("ld", ld_rec),))
        stP6.close()

        @Sy
        def _(sy):
            SM["outd"].wait(sy)

    return nc


_NC_CACHE = {}


def _get_nc(T):
    if T not in _NC_CACHE:
        _NC_CACHE[T] = build_nc(T)
    return _NC_CACHE[T]


def make_in_maps(x, Wq, bq, Wk, bk, Wv, bv, Wo, bo, w_gate, w_up, w_down, T):
    f = np.float32
    x = np.ascontiguousarray(np.asarray(x, dtype=f))
    perm = np.concatenate([np.arange(0, DH, 2), np.arange(1, DH, 2)])
    wqT = np.ascontiguousarray(np.asarray(Wq, f)[perm].T)
    wkT = np.ascontiguousarray(np.asarray(Wk, f)[perm].T)
    wvT = np.ascontiguousarray(np.asarray(Wv, f)[perm].T)
    woT = np.ascontiguousarray(np.asarray(Wo, f)[:, perm].T)
    bq_p = np.ascontiguousarray(np.asarray(bq, f)[perm].reshape(16, 128).T)
    bk_p = np.ascontiguousarray(np.asarray(bk, f)[perm].reshape(16, 128).T)
    bv_p = np.ascontiguousarray(np.asarray(bv, f)[perm].reshape(1, DH))
    bo_r = np.ascontiguousarray(np.asarray(bo, f).reshape(1, D))
    wg = np.asarray(w_gate, f)
    wu = np.asarray(w_up, f)
    wd = np.asarray(w_down, f)
    ident = np.eye(128, dtype=ml_dtypes.bfloat16)
    in_maps = []
    for c in range(8):
        seq, sl = c // 4, c % 4
        gsh = np.zeros((WSH, D), f)
        ush = np.zeros((WSH, D), f)
        dsh = np.zeros((D, WSH), f)
        lo = c * WSH
        hi = min((c + 1) * WSH, IFF)
        if hi > lo:
            gsh[0:hi - lo] = wg[lo:hi]
            ush[0:hi - lo] = wu[lo:hi]
            dsh[:, 0:hi - lo] = wd[:, lo:hi]
        in_maps.append({
            "x_own": np.ascontiguousarray(x[seq, sl * T:(sl + 1) * T]),
            "wqT": wqT, "wkT": wkT, "wvT": wvT, "woT": woT,
            "bq": bq_p, "bk": bk_p, "bv": bv_p, "bo": bo_r,
            "gate_sh": gsh, "up_sh": ush, "down_sh": dsh,
            "ident": ident,
        })
    return in_maps


def kernel(L0_attn_hidden_states, L0_attn_Wq, L0_attn_bq, L0_attn_Wk,
           L0_attn_bk, L0_attn_Wv, L0_attn_bv, L0_attn_Wo, L0_attn_bo,
           L0_ffn_w_gate, L0_ffn_w_up, L0_ffn_w_down):
    x = np.asarray(L0_attn_hidden_states, dtype=np.float32)
    B, S, _ = x.shape
    T = S // 4
    nc = _get_nc(T)
    in_maps = make_in_maps(
        x, L0_attn_Wq, L0_attn_bq, L0_attn_Wk, L0_attn_bk, L0_attn_Wv,
        L0_attn_bv, L0_attn_Wo, L0_attn_bo, L0_ffn_w_gate, L0_ffn_w_up,
        L0_ffn_w_down, T)
    res = run_bass_kernel_spmd(nc, in_maps, core_ids=list(range(8)))
    out = np.empty((B, S, D), dtype=np.float32)
    for c in range(8):
        out[c // 4, (c % 4) * T:(c % 4 + 1) * T] = res.results[c]["out"]
    return out


# version 13 (final)
# speedup vs baseline: 1.1513x; 1.0222x over previous
import numpy as np
import ml_dtypes
from contextlib import ExitStack

import concourse.bass as bass
import concourse.mybir as mybir
from concourse.bass_utils import run_bass_kernel_spmd
F32 = mybir.dt.float32
BF16 = mybir.dt.bfloat16
AX = mybir.AxisListType
ALU = mybir.AluOpType
ACT = mybir.ActivationFunctionType

D = 1024
DH = 2048
IFF = 2816
GROUPS4 = [[0, 1, 2, 3], [4, 5, 6, 7]]
GROUP8 = [list(range(8))]
WSH = 384
FFN_BLOCKS = [(r * WSH, min(WSH, IFF - r * WSH)) for r in range(8)]
NIC = IFF // 128


class Sem:
    def __init__(self, ctx, nc, name):
        self.h = ctx.enter_context(nc.semaphore(name))
        self.n = 0

    def inc(self, inst, k=1):
        inst.then_inc(self.h, k)
        self.n += k

    def dma(self, inst):
        inst.then_inc(self.h, 16)
        self.n += 16

    def wait(self, eng, val=None):
        eng.wait_ge(self.h, self.n if val is None else val)


def build_nc(T):
    NT = T // 128
    SUP = 256
    NSUP = T // SUP
    TPS = SUP // 128
    SUB = T // 4
    TPG = SUB // 128
    QB = max(128, T // 8)
    NQS = SUB // QB
    NSS = QB // 128
    NQB_OWN = T // QB
    KJ = T // 128
    NTB = T // 512

    nc = bass.Bass("TRN2", target_bir_lowering=False, debug=False,
                   detect_race_conditions=False)

    x_in = nc.dram_tensor("x_own", [T, D], F32, kind="ExternalInput").ap()
    wqT_d = nc.dram_tensor("wqT", [D, DH], F32, kind="ExternalInput").ap()
    wkT_d = nc.dram_tensor("wkT", [D, DH], F32, kind="ExternalInput").ap()
    wvT_d = nc.dram_tensor("wvT", [D, DH], F32, kind="ExternalInput").ap()
    woT_d = nc.dram_tensor("woT", [DH, D], F32, kind="ExternalInput").ap()
    bq_in = nc.dram_tensor("bq", [1, DH], F32, kind="ExternalInput").ap()
    bk_in = nc.dram_tensor("bk", [1, DH], F32, kind="ExternalInput").ap()
    bv_in = nc.dram_tensor("bv", [1, DH], F32, kind="ExternalInput").ap()
    bo_in = nc.dram_tensor("bo", [1, D], F32, kind="ExternalInput").ap()
    gsh_in = nc.dram_tensor("gate_sh", [WSH, D], F32, kind="ExternalInput").ap()
    ush_in = nc.dram_tensor("up_sh", [WSH, D], F32, kind="ExternalInput").ap()
    dsh_in = nc.dram_tensor("down_sh", [D, WSH], F32, kind="ExternalInput").ap()
    id_in = nc.dram_tensor("ident", [128, 128], BF16, kind="ExternalInput").ap()
    out_d = nc.dram_tensor("out", [T, D], F32, kind="ExternalOutput").ap()

    wg_in = nc.dram_tensor("wg_in", [D, WSH], BF16)
    wu_in = nc.dram_tensor("wu_in", [D, WSH], BF16)
    wd_in = nc.dram_tensor("wd_in", [WSH, D], BF16)
    wg_full = nc.dram_tensor("wg_full", [8 * D, WSH], BF16)
    wu_full = nc.dram_tensor("wu_full", [8 * D, WSH], BF16)
    wd_full = nc.dram_tensor("wd_full", [8 * WSH, D], BF16)
    qt_in = nc.dram_tensor("qt_in", [NQB_OWN * 16 * 128, QB], BF16)
    qt_full = nc.dram_tensor("qt_full", [4 * NQB_OWN * 16 * 128, QB], BF16)
    nrm_in = nc.dram_tensor("nrm_in", [1, 2], F32)
    nrm_out = nc.dram_tensor("nrm_out", [4, 2], F32)
    opart = [nc.dram_tensor(f"opart{g}", [4 * SUB, 2052], F32) for g in range(4)]
    ors = [nc.dram_tensor(f"ors{g}", [SUB, 2052], F32) for g in range(4)]
    x1_sp = nc.dram_tensor("x1_sp", [T, D], F32)
    x2_sp = nc.dram_tensor("x2_sp", [T, D], F32)
    ln2_sp = nc.dram_tensor("ln2_sp", [8, 128, T], BF16)

    st0 = ExitStack()
    with st0:
        sb0 = lambda n, s, d: st0.enter_context(nc.sbuf_tensor(n, s, d))
        ident = sb0("identt", [128, 128], BF16)
        ones_r = sb0("ones_r", [1, 128], BF16)
        ones_c = sb0("ones_c", [128, 1], BF16)
        bq_sb = sb0("bq_sb", [1, DH], BF16)
        bk_sb = sb0("bk_sb", [1, DH], BF16)
        bv_sb = sb0("bv_sb", [1, DH], BF16)
        bo_sb = sb0("bo_sb", [1, D], BF16)
        negm = sb0("negm", [128, 1], F32)
        mrow_b = sb0("mrow_b", [1, 128], BF16)
        ones_q = sb0("ones_q", [1, 512], BF16)
        eps_t = sb0("eps_t", [128, 1], F32)
        nrm_sb = sb0("nrm_sb", [1, 8], F32)
        nrm_pk = sb0("nrm_pk", [1, 2], F32)
        q2m = sb0("q2m", [1, 1], F32)
        k2m = sb0("k2m", [1, 1], F32)
        t11 = sb0("t11", [1, 1], F32)
        t11b = sb0("t11b", [1, 1], BF16)
        red1 = sb0("red1", [128, 1], F32)
        red2 = sb0("red2", [128, 1], F32)
        mm2_t = sb0("mm2_t", [128, 2], F32)
        cmm2_t = sb0("cmm2_t", [128, 2], F32)
        var_t = sb0("var_t", [128, 1], F32)
        rstd_t = sb0("rstd_t", [128, 1], F32)
        ln_sc = sb0("ln_sc", [128, 1024], F32)
        ln_bf = sb0("ln_bf", [128, 1024], BF16)
        si_sc = sb0("si_sc", [128, WSH], F32)
        tstage = sb0("tstage", [128, 1024], BF16)

        qbuf = {}

        def alloc_quant(stack, wmax, tag):
            sbq = lambda n, s, d: stack.enter_context(
                nc.sbuf_tensor(n + tag, s, d))
            rnb = wmax // 16
            qbuf["RNB"] = rnb
            qbuf["qs_u"] = sbq("qs_u", [128, wmax], F32)
            qbuf["qs_g"] = sbq("qs_g", [128, wmax], BF16)
            qbuf["qb1"] = sbq("qb1", [128, wmax], BF16)
            qbuf["qb2"] = sbq("qb2", [128, wmax], BF16)
            qbuf["qb3"] = sbq("qb3", [128, wmax], BF16)
            qbuf["m_sc"] = sbq("m_sc", [128, rnb], F32)
            qbuf["sc_sc"] = sbq("sc_sc", [128, 2 * rnb], F32)
            qbuf["csr_sc"] = sbq("csr_sc", [128, 2 * rnb], F32)
            qbuf["hq"] = sbq("hq", [128, wmax], BF16)

        ps_wide = st0.enter_context(nc.psum_tensor("ps_wide", [128, 2048], F32))
        ps_a = st0.enter_context(nc.psum_tensor("ps_a", [128, 512], F32))
        ps_b = st0.enter_context(nc.psum_tensor("ps_b", [128, 512], F32))
        ps_z = st0.enter_context(nc.psum_tensor("ps_z", [128, 512], F32))

        ps_tr = st0.enter_context(nc.psum_tensor("ps_tr", [128, 1024], BF16))

        names = ["ld", "ldg", "st", "cc", "sq", "var2", "rstd", "sgn", "t2d",
                 "q", "tp", "te", "gu", "si", "hmul", "dn", "x1", "wld",
                 "wld2", "vmm", "vev", "kmm", "kev", "qmm", "qev", "onm",
                 "zn", "nev", "nmr", "qld", "sT", "exp", "av", "oev",
                 "ost0", "ost1", "qst0", "qst1", "rld", "wo", "outd", "ph",
                 "scl", "sclc", "vv"]
        SM = {nm: Sem(st0, nc, nm) for nm in names}
        block = st0.enter_context(nc.Block())

        def V(f): block.vector(f)
        def Sc(f): block.scalar(f)
        def Tn(f): block.tensor(f)
        def Sy(f): block.sync(f)
        def G(f): block.gpsimd(f)

        PH = SM["ph"]
        phase_no = [0]

        def phase_barrier():
            phase_no[0] += 1
            tgt = 5 * phase_no[0]
            @V
            def _(v):
                v.sem_inc(PH.h, 1)
                v.wait_ge(PH.h, tgt)
            @Sc
            def _(s):
                s.sem_inc(PH.h, 1)
                s.wait_ge(PH.h, tgt)
            @Tn
            def _(t):
                t.sem_inc(PH.h, 1)
                t.wait_ge(PH.h, tgt)
            @Sy
            def _(s):
                s.sem_inc(PH.h, 1)
                s.wait_ge(PH.h, tgt)
            @G
            def _(g):
                g.sem_inc(PH.h, 1)
                g.wait_ge(PH.h, tgt)
            PH.n = tgt

        def emit_ln(src_ap, dst_ap, sc_waits=()):
            @V
            def _(v):
                SM["vv"].wait(v)
                for nm, val in sc_waits:
                    SM[nm].wait(v, val)
                i = v.reduce_sum(red1[:], src_ap, AX.X)
                SM["vv"].inc(i)
            @G
            def _(g):
                SM["q"].wait(g)
                for nm, val in sc_waits:
                    SM[nm].wait(g, val)
                i = g.tensor_mul(qbuf["qs_u"][:, 0:1024], src_ap, src_ap)
                SM["sq"].inc(i)
            @V
            def _(v):
                SM["vv"].wait(v)
                SM["sq"].wait(v)
                i = v.reduce_sum(red2[:], qbuf["qs_u"][:, 0:1024], AX.X)
                SM["vv"].inc(i)
                i = v.tensor_scalar_mul(mm2_t[:, 0:1], red1[:], 1.0 / D)
                SM["vv"].inc(i)
            @V
            def _(v):
                SM["vv"].wait(v)
                i = v.tensor_mul(mm2_t[:, 1:2], mm2_t[:, 0:1], mm2_t[:, 0:1])
                SM["vv"].inc(i)
            @V
            def _(v):
                SM["vv"].wait(v)
                i = v.scalar_tensor_tensor(var_t[:], red2[:], 1.0 / D,
                                           mm2_t[:, 1:2],
                                           ALU.mult, ALU.subtract)
                SM["var2"].inc(i)
            @Sc
            def _(s):
                SM["var2"].wait(s)
                i = s.activation(rstd_t[:], var_t[:], ACT.Sqrt,
                                 bias=eps_t[:, 0:1], scale=1.0)
                SM["rstd"].inc(i)
            @V
            def _(v):
                SM["vv"].wait(v)
                SM["rstd"].wait(v)
                i = v.reciprocal(mm2_t[:, 1:2], rstd_t[:])
                SM["vv"].inc(i)
            @G
            def _(g):
                SM["vv"].wait(g)
                i = g.tensor_copy(cmm2_t[:], mm2_t[:])
                SM["sclc"].inc(i)
            @V
            def _(v):
                SM["vv"].wait(v)
                SM["sclc"].wait(v)
                i = v.tensor_scalar(dst_ap, src_ap, cmm2_t[:, 0:1],
                                    cmm2_t[:, 1:2], ALU.subtract, ALU.mult)
                SM["q"].inc(i)

        def emit_quant(src_ap, out_ap, W, sc_waits=(), v_waits=()):
            nb = W // 16
            rnb = qbuf["RNB"]
            qs_u, qs_g = qbuf["qs_u"], qbuf["qs_g"]
            m_sc, sc_sc, csr_sc = qbuf["m_sc"], qbuf["sc_sc"], qbuf["csr_sc"]
            src3 = src_ap.rearrange("p (s n) -> p s n", n=16)
            @Sc
            def _(s):
                for nm, val in sc_waits:
                    SM[nm].wait(s, val)
                SM["q"].wait(s)
                i = s.sign(qs_g[:, 0:W], src_ap)
                SM["sgn"].inc(i)
                i = s.activation(qs_u[:, 0:W], src_ap, ACT.Abs)
                SM["sgn"].inc(i)
            u3 = qs_u[:, 0:W].rearrange("p (s n) -> p s n", n=16)
            u = qs_u[:, 0:W]
            b1, b2, b3 = (qbuf["qb1"][:, 0:W], qbuf["qb2"][:, 0:W],
                          qbuf["qb3"][:, 0:W])

            def vsec(*ops):
                waits, fns = [], []
                for o in ops:
                    (waits if isinstance(o, tuple) else fns).append(o)
                @V
                def _(v):
                    SM["vv"].wait(v)
                    for nm, val in waits:
                        SM[nm].wait(v, val)
                    for fn in fns:
                        i = fn(v)
                        SM["vv"].inc(i)

            vsec(*v_waits,
                 lambda v: v.tensor_reduce(m_sc[:, 0:nb], src3, AX.X, ALU.max,
                                           apply_absolute_value=True))
            vsec(lambda v: v.tensor_scalar(sc_sc[:, 0:nb], m_sc[:, 0:nb],
                                           1.0 / 6.0, 1e-12, ALU.mult,
                                           ALU.max))
            vsec(lambda v: v.reciprocal(sc_sc[:, rnb:rnb + nb], sc_sc[:, 0:nb]))
            @G
            def _(g):
                SM["vv"].wait(g)
                i = g.tensor_copy(csr_sc[:, 0:nb], sc_sc[:, 0:nb])
                SM["sclc"].inc(i)
                i = g.tensor_copy(csr_sc[:, rnb:rnb + nb],
                                  sc_sc[:, rnb:rnb + nb])
                SM["sclc"].inc(i)
            rb = csr_sc[:, rnb:rnb + nb].unsqueeze(2).broadcast_to(
                [128, nb, 16])
            scb = csr_sc[:, 0:nb].unsqueeze(2).broadcast_to([128, nb, 16])
            vsec(("sclc", None), ("tp", None), ("sgn", None),
                 lambda v: v.tensor_tensor(u3, u3, rb, ALU.mult))
            vsec(lambda v: v.tensor_scalar(b1, u, 0.25, 0.5, ALU.is_gt, ALU.mult),
                 lambda v: v.tensor_scalar(b2, u, 0.75, 0.5, ALU.is_gt, ALU.mult),
                 lambda v: v.tensor_scalar(b3, u, 1.25, 0.5, ALU.is_gt, ALU.mult),
                 lambda v: v.tensor_scalar(out_ap, u, 1.75, 0.5, ALU.is_gt, ALU.mult))
            vsec(lambda v: v.tensor_add(b1, b1, b2),
                 lambda v: v.tensor_add(b3, b3, out_ap))
            vsec(lambda v: v.tensor_scalar(b2, u, 2.5, 1.0, ALU.is_gt, ALU.mult),
                 lambda v: v.tensor_scalar(out_ap, u, 3.5, 1.0, ALU.is_gt, ALU.mult))
            vsec(lambda v: v.tensor_add(b1, b1, b3),
                 lambda v: v.tensor_add(b2, b2, out_ap))
            vsec(lambda v: v.tensor_scalar(b3, u, 5.0, 2.0, ALU.is_gt, ALU.mult),
                 lambda v: v.tensor_add(b1, b1, b2))
            vsec(lambda v: v.tensor_add(b1, b1, b3))
            vsec(lambda v: v.tensor_mul(u, b1, qs_g[:, 0:W]))
            @V
            def _(v):
                SM["vv"].wait(v)
                SM["sclc"].wait(v)
                i = v.tensor_tensor(
                    out_ap.rearrange("p (s n) -> p s n", n=16),
                    u3, scb, ALU.mult)
                SM["q"].inc(i)

        tp_slot = [0]

        def emit_transpose(src_ap, col0, nchunk, dst_ap, tn_waits=()):
            slot = tp_slot[0] % 2
            tp_slot[0] += 1
            my_te = SM["te"].n + 1
            @Tn
            def _(t):
                if tp_slot[0] > 2:
                    SM["te"].wait(t, my_te - 2)
                SM["q"].wait(t)
                for nm, val in tn_waits:
                    SM[nm].wait(t, val)
                i = None
                for c in range(nchunk):
                    i = t.transpose(
                        ps_tr[:, slot * 512 + c * 128:slot * 512 + (c + 1) * 128],
                        src_ap[:, col0 + c * 128:col0 + (c + 1) * 128],
                        ident[:])
                SM["tp"].inc(i)
            @Sc
            def _(s):
                SM["tp"].wait(s)
                i = s.copy(dst_ap,
                           ps_tr[:, slot * 512:slot * 512 + nchunk * 128]
                           .rearrange("p (c x) -> p c x", c=nchunk))
                SM["te"].inc(i)

        @Sy
        def _(sy):
            SM["ld"].dma(sy.dma_start(out=ident[:], in_=id_in))
        @G
        def _(g):
            g.memset(ones_r[:], 1.0)
            g.memset(ones_c[:], 1.0)
            g.memset(ones_q[:], 1.0)
            SM["ldg"].dma(g.dma_start(out=bv_sb[:], in_=bv_in))
            SM["ldg"].dma(g.dma_start(out=bo_sb[:], in_=bo_in))
            SM["ldg"].dma(g.dma_start(out=bq_sb[:], in_=bq_in))
            SM["ldg"].dma(g.dma_start(out=bk_sb[:], in_=bk_in))
        @V
        def _(v):
            v.memset(q2m[:], 0.0)
            v.memset(k2m[:], 0.0)
            v.memset(eps_t[:], 1e-5)

        stW = ExitStack()
        alloc_quant(stW, 1024, "W")
        wsh_g = stW.enter_context(nc.sbuf_tensor("wsh_g", [128, 3 * 1024], F32))
        wsh_u = stW.enter_context(nc.sbuf_tensor("wsh_u", [128, 3 * 1024], F32))
        wsh_d = stW.enter_context(nc.sbuf_tensor("wsh_d", [128, 8 * WSH], F32))
        wq_bf = stW.enter_context(nc.sbuf_tensor("wq_bf", [128, 1024], BF16))
        @Sy
        def _(sy):
            SM["ld"].dma(sy.dma_start(
                out=wsh_g[:].rearrange("p (t d) -> p t d", t=3),
                in_=gsh_in.rearrange("(t p) d -> p t d", p=128)))
            SM["ld"].dma(sy.dma_start(
                out=wsh_u[:].rearrange("p (t d) -> p t d", t=3),
                in_=ush_in.rearrange("(t p) d -> p t d", p=128)))
            SM["ld"].dma(sy.dma_start(
                out=wsh_d[:].rearrange("p (t w) -> p t w", t=8),
                in_=dsh_in.rearrange("(t p) w -> p t w", p=128)))
        @V
        def _(v):
            SM["ld"].wait(v)
        @Sc
        def _(s):
            SM["ld"].wait(s)
            SM["ldg"].wait(s)
        @Tn
        def _(t):
            SM["ld"].wait(t)
            SM["ldg"].wait(t)

        def w_qts(src_sb, tile_i, W, dst_dram_ap):
            nchunk = W // 128
            src = src_sb[:, tile_i * W:(tile_i + 1) * W]
            emit_quant(src, wq_bf[:, 0:W], W)
            for h in range(0, nchunk, 4):
                nch = min(4, nchunk - h)
                emit_transpose(
                    wq_bf[:, 0:W], h * 128, nch,
                    tstage[:, 0:nch * 128].rearrange("p (c x) -> p c x", c=nch))
                @Sy
                def _(sy, h=h, nch=nch):
                    SM["te"].wait(sy)
                    SM["st"].dma(sy.dma_start(
                        out=dst_dram_ap[h:h + nch].transpose([1, 0, 2]),
                        in_=tstage[:, 0:nch * 128]
                        .rearrange("p (c x) -> p c x", c=nch)))
                @Sc
                def _(s):
                    SM["st"].wait(s)

        for src_sb, bounce in ((wsh_g, wg_in), (wsh_u, wu_in)):
            for t3 in range(3):
                dst = bounce.ap().rearrange(
                    "(c p) (t x) -> c p t x", p=128, x=128)[:, :, t3, :]
                w_qts(src_sb, t3, 1024, dst)
        for t8 in range(8):
            dst = wd_in.ap().rearrange(
                "(c p) (t x) -> c p t x", p=128, x=128)[:, :, t8, :]
            w_qts(wsh_d, t8, WSH, dst)

        @G
        def _(g):
            SM["st"].wait(g)
            for a, b in ((wg_in, wg_full), (wu_in, wu_full), (wd_in, wd_full)):
                i = g.collective_compute(
                    "AllGather", ALU.bypass, replica_groups=GROUP8,
                    ins=[a.ap().opt()], outs=[b.ap().opt()])
                SM["cc"].inc(i)
        CC_W = SM["cc"].n

        stW.close()
        phase_barrier()

        wstate = {"wfree": [], "wdfree": []}

        def emit_ffn(x_sb, ntile, base_tok, bufs, ln2_spill, out_write,
                     x_ready):
            STW = ntile * 128
            h_full = bufs["h"]
            xqT = bufs["xqT"][:].rearrange("p (c t) -> p c t", c=8)
            hqT = bufs["hqT"][:].rearrange("p (c t) -> p c t", c=NIC)
            wg3 = bufs["wg"][:].rearrange("p (b c w) -> p b c w", b=2, c=8)
            wu3 = bufs["wu"][:].rearrange("p (b c w) -> p b c w", b=2, c=8)
            wd3 = bufs["wd"][:].rearrange("p (c d) -> p c d", c=NIC)

            for tl in range(ntile):
                xt = x_sb[:, tl * 1024:(tl + 1) * 1024]
                emit_ln(xt, ln_sc[:], sc_waits=x_ready)
                emit_quant(ln_sc[:], ln_bf[:], 1024,
                           v_waits=(("q", SM["q"].n),))
                for h in (0, 4):
                    emit_transpose(ln_bf[:], h * 128, 4,
                                   xqT[:, h:h + 4, tl * 128:(tl + 1) * 128])

            @Sy
            def _(sy):
                sy.wait_ge(SM["cc"].h, CC_W)
                if wstate["wdfree"]:
                    SM["dn"].wait(sy, wstate["wdfree"][-1])
                SM["wld2"].dma(sy.dma_start(
                    out=wd3,
                    in_=wd_full.ap().rearrange(
                        "(c p) d -> p c d", p=128)[:, 0:NIC, :]))

            hm_last = [0] * ntile
            for r, (off, w) in enumerate(FFN_BLOCKS):
                par = len(wstate["wfree"]) % 2
                @Sy
                def _(sy, r=r, par=par, w=w):
                    sy.wait_ge(SM["cc"].h, CC_W)
                    hist = wstate["wfree"]
                    if len(hist) >= 2:
                        SM["gu"].wait(sy, hist[-2])
                    SM["wld"].dma(sy.dma_start(
                        out=wg3[:, par, :, 0:w],
                        in_=wg_full.ap().rearrange(
                            "(r c p) w -> r p c w", c=8, p=128)[r][:, :, 0:w]))
                    SM["wld"].dma(sy.dma_start(
                        out=wu3[:, par, :, 0:w],
                        in_=wu_full.ap().rearrange(
                            "(r c p) w -> r p c w", c=8, p=128)[r][:, :, 0:w]))
                for tl in range(ntile):
                    @Tn
                    def _(t, par=par, w=w, tl=tl):
                        SM["wld"].wait(t)
                        SM["te"].wait(t)
                        SM["si"].wait(t)
                        SM["hmul"].wait(t)
                        i = None
                        for fc in range(8):
                            i = t.matmul(ps_a[:, 0:w],
                                         xqT[:, fc, tl * 128:(tl + 1) * 128],
                                         wg3[:, par, fc, 0:w],
                                         start=(fc == 0), stop=(fc == 7))
                        SM["gu"].inc(i)
                        for fc in range(8):
                            i = t.matmul(ps_b[:, 0:w],
                                         xqT[:, fc, tl * 128:(tl + 1) * 128],
                                         wu3[:, par, fc, 0:w],
                                         start=(fc == 0), stop=(fc == 7))
                        SM["gu"].inc(i)
                    @Sc
                    def _(s, w=w):
                        SM["gu"].wait(s)
                        i = s.activation(si_sc[:, 0:w], ps_a[:, 0:w], ACT.Silu)
                        SM["si"].inc(i)
                    @V
                    def _(v, w=w, off=off, tl=tl):
                        SM["si"].wait(v)
                        i = v.tensor_tensor(
                            h_full[:, tl * IFF + off:tl * IFF + off + w],
                            si_sc[:, 0:w], ps_b[:, 0:w], ALU.mult)
                        SM["hmul"].inc(i)
                    hm_last[tl] = SM["hmul"].n
                wstate["wfree"].append(SM["gu"].n)

            for tl in range(ntile):
                emit_quant(h_full[:, tl * IFF:(tl + 1) * IFF],
                           qbuf["hq"][:, 0:IFF], IFF,
                           sc_waits=(("hmul", hm_last[tl]),),
                           v_waits=(("hmul", hm_last[tl]),))
                for hh in range(0, NIC, 4):
                    nch = min(4, NIC - hh)
                    emit_transpose(qbuf["hq"][:, 0:IFF], hh * 128, nch,
                                   hqT[:, hh:hh + nch,
                                       tl * 128:(tl + 1) * 128])

            for tl in range(ntile):
                @Tn
                def _(t, tl=tl):
                    SM["wld2"].wait(t)
                    SM["te"].wait(t)
                    SM["x1"].wait(t)
                    i = None
                    for ic in range(NIC):
                        stat = hqT[:, ic, tl * 128:(tl + 1) * 128]
                        i = t.matmul(ps_wide[:, 0:512], stat, wd3[:, ic, 0:512],
                                     start=(ic == 0), stop=(ic == NIC - 1))
                        i = t.matmul(ps_wide[:, 512:1024], stat,
                                     wd3[:, ic, 512:1024],
                                     start=(ic == 0), stop=(ic == NIC - 1))
                    SM["dn"].inc(i)
                x1_rec = SM["x1"].n + 1
                @V
                def _(v, tl=tl):
                    SM["x1"].wait(v)
                    SM["dn"].wait(v)
                    xt = x_sb[:, tl * 1024:(tl + 1) * 1024]
                    i = v.scalar_tensor_tensor(xt, ps_wide[:, 0:1024], 0.5,
                                               xt, ALU.mult, ALU.add)
                    SM["x1"].inc(i)
                if ln2_spill:
                    xt = x_sb[:, tl * 1024:(tl + 1) * 1024]
                    tok = base_tok + tl * 128
                    emit_ln(xt, ln_bf[:], sc_waits=(("x1", x1_rec),))
                    for h in (0, 4):
                        emit_transpose(
                            ln_bf[:], h * 128, 4,
                            tstage[:, 0:512].rearrange("p (c x) -> p c x", c=4),
                            tn_waits=(("st", SM["st"].n),))
                        @Sy
                        def _(sy, h=h, tok=tok):
                            SM["te"].wait(sy)
                            SM["st"].dma(sy.dma_start(
                                out=ln2_sp.ap()[h:h + 4, :, tok:tok + 128]
                                .transpose([1, 0, 2]),
                                in_=tstage[:, 0:512]
                                .rearrange("p (c x) -> p c x", c=4)))
                    @Sy
                    def _(sy, tl=tl, tok=tok):
                        SM["x1"].wait(sy)
                        SM["st"].dma(sy.dma_start(
                            out=x1_sp.ap()[tok:tok + 128, :],
                            in_=x_sb[:, tl * 1024:(tl + 1) * 1024]))
                if out_write is not None:
                    out_write(tl)
            wstate["wdfree"].append(SM["dn"].n)

        stF1 = ExitStack()
        sbF = lambda n, s, d: stF1.enter_context(nc.sbuf_tensor(n, s, d))
        alloc_quant(stF1, IFF, "F")
        x_st = sbF("x_st", [128, TPS * 1024], F32)
        f1bufs = {
            "xqT": sbF("xqT1", [128, 8 * SUP], BF16),
            "hqT": sbF("hqT1", [128, NIC * SUP], BF16),
            "h": sbF("h1", [128, TPS * IFF], BF16),
            "wg": sbF("wg1", [128, 2 * 8 * WSH], BF16),
            "wu": sbF("wu1", [128, 2 * 8 * WSH], BF16),
            "wd": sbF("wd1", [128, NIC * 1024], BF16),
        }
        for sti in range(NSUP):
            @Sy
            def _(sy, sti=sti):
                if sti > 0:
                    SM["x1"].wait(sy)
                    SM["sq"].wait(sy)
                    SM["st"].wait(sy)
                SM["ld"].dma(sy.dma_start(
                    out=x_st[:].rearrange("p (t d) -> p t d", t=TPS),
                    in_=x_in.rearrange("(t p) d -> p t d", p=128)
                    [:, sti * TPS:(sti + 1) * TPS, :]))
            ld_rec = SM["ld"].n
            emit_ffn(x_st[:], TPS, sti * SUP, f1bufs, ln2_spill=True,
                     out_write=None, x_ready=(("ld", ld_rec),))

        stF1.close()
        phase_barrier()

        stP = ExitStack()
        sbP = lambda n, s, d: stP.enter_context(nc.sbuf_tensor(n, s, d))
        K_sb = sbP("K_sb", [128, 16 * T], BF16)
        V_sb = sbP("V_sb", [128, KJ * 2048], BF16)
        K3 = K_sb[:].rearrange("p (c k) -> p c k", c=16)
        V3 = V_sb[:].rearrange("p (j d) -> p j d", j=KJ)
        stP3 = ExitStack()
        sbP3 = lambda n, s, d: stP3.enter_context(nc.sbuf_tensor(n, s, d))
        aw_sb = sbP3("aw_sb", [128, 8 * 1024], BF16)
        aw3 = aw_sb[:].rearrange("p (c d) -> p c d", c=8)
        l2b = sbP3("l2b", [128, 8 * 512], BF16)
        l2b3 = l2b[:].rearrange("p (c t) -> p c t", c=8)
        qstage = sbP3("qstage", [128, 2 * 512], BF16)
        sqst = sbP3("sqst", [128, 512], BF16)
        q2part = sbP3("q2part", [1, T], F32)
        k2part = sbP3("k2part", [1, T], F32)

        def load_ln2(tb, waits):
            @Sy
            def _(sy, tb=tb):
                for nm, val in waits:
                    SM[nm].wait(sy, val)
                SM["ld"].dma(sy.dma_start(
                    out=l2b3,
                    in_=ln2_sp.ap()[:, :, tb * 512:(tb + 1) * 512]
                    .transpose([1, 0, 2])))

        def load_awhalf(src_dram, half, waits):
            @G
            def _(g, half=half):
                for nm, val in waits:
                    SM[nm].wait(g, val)
                SM["ldg"].dma(g.dma_start(
                    out=aw3,
                    in_=src_dram.rearrange("(c p) e -> p c e", p=128)
                    [:, :, half * 1024:(half + 1) * 1024]))

        qcnt = [0]
        for half in range(2):
            load_awhalf(wqT_d, half,
                        [("qmm", SM["qmm"].n)] if half else [])
            for tb in range(NTB):
                load_ln2(tb, [("qmm", SM["qmm"].n)] if (half or tb) else [])
                for dch in range(8):
                    dc = half * 8 + dch
                    pp = ps_a if (dch % 2 == 0) else ps_b
                    par = qcnt[0] % 2
                    qcnt[0] += 1
                    @Tn
                    def _(t, dc=dc, dch=dch, pp=pp):
                        SM["ld"].wait(t)
                        SM["ldg"].wait(t)
                        SM["qev"].wait(t, max(0, SM["qev"].n - 1))
                        for fc in range(8):
                            t.matmul(pp[:, 0:512],
                                     aw3[:, fc, dch * 128:dch * 128 + 128],
                                     l2b3[:, fc, :],
                                     start=(fc == 0), stop=False)
                        i = t.matmul(pp[:, 0:512],
                                     bq_sb[:, dc * 128:(dc + 1) * 128],
                                     ones_q[:, 0:512],
                                     start=False, stop=True)
                        SM["qmm"].inc(i)
                    @V
                    def _(v, pp=pp, par=par):
                        SM["qmm"].wait(v)
                        SM[f"qst{par}"].wait(v)
                        i = v.tensor_copy(qstage[:, par * 512:(par + 1) * 512],
                                          pp[:, 0:512])
                        SM["qev"].inc(i)
                    @V
                    def _(v, par=par):
                        SM["qev"].wait(v)
                        i = v.tensor_mul(sqst[:],
                                         qstage[:, par * 512:(par + 1) * 512],
                                         qstage[:, par * 512:(par + 1) * 512])
                        SM["onm"].inc(i)
                    @Tn
                    def _(t, dch=dch):
                        SM["onm"].wait(t)
                        if dch == 0:
                            SM["nev"].wait(t)
                        i = t.matmul(ps_z[0:1, 0:512], ones_c[:], sqst[:],
                                     start=(dch == 0), stop=(dch == 7))
                        if dch == 7:
                            SM["zn"].inc(i)
                    @Sy
                    def _(sy, tb=tb, dc=dc, par=par):
                        SM["qev"].wait(sy)
                        for s2 in range(512 // QB):
                            qb = (tb * 512) // QB + s2
                            SM[f"qst{par}"].dma(sy.dma_start(
                                out=qt_in.ap()[qb * 2048 + dc * 128:
                                               qb * 2048 + dc * 128 + 128, :],
                                in_=qstage[:, par * 512 + s2 * QB:
                                           par * 512 + (s2 + 1) * QB]))
                if half == 0:
                    @V
                    def _(v, tb=tb):
                        SM["nev"].wait(v)
                        SM["zn"].wait(v)
                        i = v.tensor_copy(q2part[0:1, tb * 512:(tb + 1) * 512],
                                          ps_z[0:1, 0:512])
                        SM["nev"].inc(i)
                else:
                    @V
                    def _(v, tb=tb):
                        SM["nev"].wait(v)
                        SM["zn"].wait(v)
                        i = v.tensor_add(q2part[0:1, tb * 512:(tb + 1) * 512],
                                         q2part[0:1, tb * 512:(tb + 1) * 512],
                                         ps_z[0:1, 0:512])
                        SM["nev"].inc(i)
                    @V
                    def _(v, tb=tb):
                        SM["nev"].wait(v)
                        i = v.reduce_max(t11[:],
                                         q2part[0:1, tb * 512:(tb + 1) * 512],
                                         AX.X)
                        SM["nev"].inc(i)
                    @V
                    def _(v):
                        SM["nev"].wait(v)
                        i = v.tensor_max(q2m[:], q2m[:], t11[:])
                        SM["nev"].inc(i)

        @G
        def _(g):
            SM["qst0"].wait(g)
            SM["qst1"].wait(g)
            i = g.collective_compute(
                "AllGather", ALU.bypass, replica_groups=GROUPS4,
                ins=[qt_in.ap().opt()], outs=[qt_full.ap().opt()])
            SM["cc"].inc(i)
        CC_QT = SM["cc"].n

        for half in range(2):
            load_awhalf(wkT_d, half, [("kmm", SM["kmm"].n)] if half
                        else [("qmm", SM["qmm"].n)])
            for kb in range(NTB):
                load_ln2(kb, [("kmm", SM["kmm"].n)] if (half or kb)
                         else [("qmm", SM["qmm"].n)])
                for dch in range(8):
                    dc = half * 8 + dch
                    pp = ps_a if (dch % 2 == 0) else ps_b
                    @Tn
                    def _(t, dc=dc, dch=dch, pp=pp):
                        SM["ld"].wait(t)
                        SM["ldg"].wait(t)
                        SM["kev"].wait(t, max(0, SM["kev"].n - 1))
                        for fc in range(8):
                            t.matmul(pp[:, 0:512],
                                     aw3[:, fc, dch * 128:dch * 128 + 128],
                                     l2b3[:, fc, :],
                                     start=(fc == 0), stop=False)
                        i = t.matmul(pp[:, 0:512],
                                     bk_sb[:, dc * 128:(dc + 1) * 128],
                                     ones_q[:, 0:512],
                                     start=False, stop=True)
                        SM["kmm"].inc(i)
                    @V
                    def _(v, dc=dc, kb=kb, pp=pp):
                        SM["kmm"].wait(v)
                        i = v.tensor_copy(K3[:, dc, kb * 512:(kb + 1) * 512],
                                          pp[:, 0:512])
                        SM["kev"].inc(i)
                    @V
                    def _(v, dc=dc, kb=kb):
                        SM["kev"].wait(v)
                        i = v.tensor_mul(sqst[:],
                                         K3[:, dc, kb * 512:(kb + 1) * 512],
                                         K3[:, dc, kb * 512:(kb + 1) * 512])
                        SM["onm"].inc(i)
                    @Tn
                    def _(t, dch=dch):
                        SM["onm"].wait(t)
                        if dch == 0:
                            SM["nev"].wait(t)
                        i = t.matmul(ps_z[0:1, 0:512], ones_c[:], sqst[:],
                                     start=(dch == 0), stop=(dch == 7))
                        if dch == 7:
                            SM["zn"].inc(i)
                if half == 0:
                    @V
                    def _(v, kb=kb):
                        SM["nev"].wait(v)
                        SM["zn"].wait(v)
                        i = v.tensor_copy(k2part[0:1, kb * 512:(kb + 1) * 512],
                                          ps_z[0:1, 0:512])
                        SM["nev"].inc(i)
                else:
                    @V
                    def _(v, kb=kb):
                        SM["nev"].wait(v)
                        SM["zn"].wait(v)
                        i = v.tensor_add(k2part[0:1, kb * 512:(kb + 1) * 512],
                                         k2part[0:1, kb * 512:(kb + 1) * 512],
                                         ps_z[0:1, 0:512])
                        SM["nev"].inc(i)
                    @V
                    def _(v, kb=kb):
                        SM["nev"].wait(v)
                        i = v.reduce_max(t11[:],
                                         k2part[0:1, kb * 512:(kb + 1) * 512],
                                         AX.X)
                        SM["nev"].inc(i)
                    @V
                    def _(v):
                        SM["nev"].wait(v)
                        i = v.tensor_max(k2m[:], k2m[:], t11[:])
                        SM["nev"].inc(i)

        @V
        def _(v):
            SM["nev"].wait(v)
            i = v.tensor_copy(nrm_pk[0:1, 0:1], q2m[:])
            SM["onm"].inc(i)
            i = v.tensor_copy(nrm_pk[0:1, 1:2], k2m[:])
            SM["onm"].inc(i)
        @Sy
        def _(sy):
            SM["onm"].wait(sy)
            SM["st"].dma(sy.dma_start(out=nrm_in.ap(), in_=nrm_pk[:]))
        @G
        def _(g):
            SM["st"].wait(g)
            i = g.collective_compute(
                "AllGather", ALU.bypass, replica_groups=GROUPS4,
                ins=[nrm_in.ap().opt()], outs=[nrm_out.ap().opt()])
            SM["cc"].inc(i)
        CC_NRM = SM["cc"].n
        @Sy
        def _(sy):
            sy.wait_ge(SM["cc"].h, CC_NRM)
            SM["ld"].dma(sy.dma_start(
                out=nrm_sb[:], in_=nrm_out.ap().rearrange("a b -> (a b)").unsqueeze(0)))
        @V
        def _(v):
            SM["vv"].wait(v)
            SM["ld"].wait(v)
            i = v.tensor_reduce(nrm_pk[0:1, 0:2],
                                nrm_sb[:].rearrange("p (a b) -> p b a", b=2),
                                AX.X, ALU.max)
            SM["vv"].inc(i)
        @V
        def _(v):
            SM["vv"].wait(v)
            i = v.tensor_mul(t11[:], nrm_pk[0:1, 0:1], nrm_pk[0:1, 1:2])
            SM["onm"].inc(i)
        @Sc
        def _(s):
            SM["onm"].wait(s)
            i = s.activation(rstd_t[0:1, 0:1], t11[:], ACT.Sqrt)
            SM["rstd"].inc(i)
        @V
        def _(v):
            SM["rstd"].wait(v)
            i = v.tensor_scalar_mul(t11b[:], rstd_t[0:1, 0:1], -1.0)
            SM["onm"].inc(i)
        @Tn
        def _(t):
            SM["onm"].wait(t)
            SM["nev"].wait(t)
            i = t.matmul(ps_z[0:1, 0:128], t11b[:], ones_r[:],
                         start=True, stop=True)
            SM["zn"].inc(i)
        @V
        def _(v):
            SM["zn"].wait(v)
            i = v.tensor_copy(mrow_b[:], ps_z[0:1, 0:128])
            SM["nmr"].inc(i)
        NMR = SM["nmr"].n

        for half in range(2):
            load_awhalf(wvT_d, half, [("vmm", SM["vmm"].n)] if half
                        else [("kmm", SM["kmm"].n)])
            for tb in range(NTB):
                load_ln2(tb, [("vmm", SM["vmm"].n)] if (half or tb)
                         else [("kmm", SM["kmm"].n)])
                for tw in range(4):
                    tl = tb * 4 + tw
                    @Tn
                    def _(t, half=half, tw=tw):
                        SM["ld"].wait(t)
                        SM["ldg"].wait(t)
                        SM["vev"].wait(t)
                        i = None
                        for n in range(2):
                            nn = half * 1024 + n * 512
                            for fc in range(8):
                                i = t.matmul(
                                    ps_wide[:, n * 512:(n + 1) * 512],
                                    l2b3[:, fc, tw * 128:(tw + 1) * 128],
                                    aw3[:, fc, n * 512:(n + 1) * 512],
                                    start=(fc == 0), stop=False)
                            i = t.matmul(ps_wide[:, n * 512:(n + 1) * 512],
                                         ones_r[:], bv_sb[:, nn:nn + 512],
                                         start=False, stop=True)
                        SM["vmm"].inc(i)
                    @V
                    def _(v, half=half, tl=tl):
                        SM["vmm"].wait(v)
                        i = v.tensor_copy(V3[:, tl, half * 1024:(half + 1) * 1024],
                                          ps_wide[:, 0:1024])
                        SM["vev"].inc(i)

        phase_barrier()

        stP3.close()
        qtb = sbP("qtb", [128, 2 * 16 * QB], BF16)
        qtb4 = qtb[:].rearrange("p (b c q) -> p b c q", b=2, c=16)
        ptb = sbP("ptb", [128, KJ * QB], BF16)
        ptb3 = ptb[:].rearrange("p (j q) -> p j q", j=KJ)
        ostage = sbP("ostage", [128, 2 * 2052], F32)

        @V
        def _(v):
            v.memset(ostage[:, 2049:2052], 0.0)
            v.memset(ostage[:, 2052 + 2049:2052 + 2052], 0.0)

        qfree = [0, 0]
        exp_par = [0, 0]
        rs_cc = []
        qn = [0]
        on = [0]
        for g in range(4):
            for o in range(4):
                for qs2 in range(NQS):
                    qb_own = g * NQS + qs2
                    qb_glob = o * NQB_OWN + qb_own
                    par = qn[0] % 2
                    qn[0] += 1
                    @Sy
                    def _(sy, qb_glob=qb_glob, par=par):
                        sy.wait_ge(SM["cc"].h, CC_QT)
                        SM["sT"].wait(sy, qfree[par])
                        SM["qld"].dma(sy.dma_start(
                            out=qtb4[:, par],
                            in_=qt_full.ap().rearrange(
                                "(b c p) q -> b p c q", c=16, p=128)[qb_glob]))
                    for j in range(KJ):
                        pp = ps_a if (j % 2 == 0) else ps_b
                        ppi = j % 2
                        @Tn
                        def _(t, j=j, pp=pp, ppi=ppi, par=par):
                            SM["qld"].wait(t)
                            t.wait_ge(SM["nmr"].h, NMR)
                            SM["exp"].wait(t, exp_par[ppi])
                            for dc in range(16):
                                t.matmul(pp[:, 0:QB],
                                         K3[:, dc, j * 128:(j + 1) * 128],
                                         qtb4[:, par, dc, :],
                                         start=(dc == 0), stop=False)
                            i = t.matmul(pp[:, 0:QB], mrow_b[:, j * 0:128],
                                         ones_q[:, 0:QB],
                                         start=False, stop=True)
                            SM["sT"].inc(i)
                        @Sc
                        def _(s, j=j, pp=pp):
                            SM["sT"].wait(s)
                            i = s.activation(ptb3[:, j, :], pp[:, 0:QB],
                                             ACT.Exp, bias=0.0,
                                             scale=1.0 / 32.0)
                            SM["exp"].inc(i)
                        exp_par[ppi] = SM["exp"].n
                    for ss in range(NSS):
                        opar = on[0] % 2
                        on[0] += 1
                        @Tn
                        def _(t, ss=ss):
                            SM["exp"].wait(t)
                            SM["oev"].wait(t)
                            i = None
                            for j in range(KJ):
                                stat = ptb3[:, j, ss * 128:(ss + 1) * 128]
                                for n in range(4):
                                    t.matmul(ps_wide[:, n * 512:(n + 1) * 512],
                                             stat, V3[:, j, n * 512:(n + 1) * 512],
                                             start=(j == 0), stop=(j == KJ - 1))
                                i = t.matmul(ps_z[:, ss:ss + 1], stat, ones_c[:],
                                             start=(j == 0), stop=(j == KJ - 1))
                            SM["av"].inc(i)
                        @V
                        def _(v, ss=ss, opar=opar):
                            SM["av"].wait(v)
                            SM[f"ost{opar}"].wait(v)
                            i = v.tensor_copy(
                                ostage[:, opar * 2052:opar * 2052 + 2048],
                                ps_wide[:])
                            SM["oev"].inc(i)
                            i = v.tensor_copy(
                                ostage[:, opar * 2052 + 2048:opar * 2052 + 2049],
                                ps_z[:, ss:ss + 1])
                            SM["oev"].inc(i)
                        @Sy
                        def _(sy, g=g, o=o, qs2=qs2, ss=ss, opar=opar):
                            SM["oev"].wait(sy)
                            row0 = o * SUB + qs2 * QB + ss * 128
                            SM[f"ost{opar}"].dma(sy.dma_start(
                                out=opart[g].ap()[row0:row0 + 128, :],
                                in_=ostage[:, opar * 2052:(opar + 1) * 2052]))
                    qfree[par] = SM["sT"].n
            @G
            def _(g_, g=g):
                SM["ost0"].wait(g_)
                SM["ost1"].wait(g_)
                i = g_.collective_compute(
                    "ReduceScatter", ALU.add, replica_groups=GROUPS4,
                    ins=[opart[g].ap().opt()], outs=[ors[g].ap().opt()])
                SM["cc"].inc(i)
            rs_cc.append(SM["cc"].n)

        stP.close()
        phase_barrier()

        stP5 = ExitStack()
        sbQ = lambda n, s, d: stP5.enter_context(nc.sbuf_tensor(n, s, d))
        woT_sb = sbQ("woT_sb", [128, 16 * 1024], BF16)
        woT3 = woT_sb[:].rearrange("p (c d) -> p c d", c=16)
        rsb = sbQ("rsb", [128, 2052], F32)
        x_g = sbQ("x_g", [128, TPG * 1024], F32)
        obf = sbQ("obf", [128, 2048], BF16)
        oT = sbQ("oT", [128, 16 * 128], BF16)
        oT3 = oT[:].rearrange("p (c x) -> p c x", c=16)
        rz_t = sbQ("rz_t", [128, 1], F32)
        @G
        def _(g):
            SM["ldg"].dma(g.dma_start(
                out=woT3, in_=woT_d.rearrange("(c p) d -> p c d", p=128)))
        @Sy
        def _(sy):
            SM["st"].wait(sy)

        for g in range(4):
            for tl in range(TPG):
                tok = g * SUB + tl * 128
                @Sy
                def _(sy, g=g, tl=tl, tok=tok):
                    sy.wait_ge(SM["cc"].h, rs_cc[g])
                    SM["q"].wait(sy)
                    SM["st"].wait(sy)
                    SM["rld"].dma(sy.dma_start(
                        out=rsb[:], in_=ors[g].ap()[tl * 128:(tl + 1) * 128, :]))
                    SM["rld"].dma(sy.dma_start(
                        out=x_g[:, tl * 1024:(tl + 1) * 1024],
                        in_=x1_sp.ap()[tok:tok + 128, :]))
                @V
                def _(v):
                    SM["rld"].wait(v)
                    i = v.reciprocal(rz_t[:], rsb[:, 2048:2049])
                    SM["scl"].inc(i)
                @Sc
                def _(s):
                    SM["scl"].wait(s)
                    i = s.mul(obf[:], rsb[:, 0:2048], rz_t[:, 0:1])
                    SM["q"].inc(i)
                for h in (0, 4, 8, 12):
                    emit_transpose(obf[:], h * 128, 4, oT3[:, h:h + 4, :])
                @Tn
                def _(t, tl=tl):
                    SM["te"].wait(t)
                    SM["ldg"].wait(t)
                    SM["x1"].wait(t)
                    i = None
                    for n in range(2):
                        for dc in range(16):
                            i = t.matmul(ps_wide[:, n * 512:(n + 1) * 512],
                                         oT3[:, dc, :],
                                         woT3[:, dc, n * 512:(n + 1) * 512],
                                         start=(dc == 0), stop=False)
                        i = t.matmul(ps_wide[:, n * 512:(n + 1) * 512],
                                     ones_r[:], bo_sb[:, n * 512:(n + 1) * 512],
                                     start=False, stop=True)
                    SM["wo"].inc(i)
                @V
                def _(v, tl=tl):
                    SM["wo"].wait(v)
                    SM["rld"].wait(v)
                    xt = x_g[:, tl * 1024:(tl + 1) * 1024]
                    i = v.tensor_tensor(xt, ps_wide[:, 0:1024], xt, ALU.add)
                    SM["x1"].inc(i)
                @Sy
                def _(sy, tl=tl, tok=tok):
                    SM["x1"].wait(sy)
                    SM["st"].dma(sy.dma_start(
                        out=x2_sp.ap()[tok:tok + 128, :],
                        in_=x_g[:, tl * 1024:(tl + 1) * 1024]))

        stP5.close()
        phase_barrier()

        stP6 = ExitStack()
        sbG = lambda n, s, d: stP6.enter_context(nc.sbuf_tensor(n, s, d))
        alloc_quant(stP6, IFF, "G")
        x_st2 = sbG("x_st2", [128, TPS * 1024], F32)
        f2bufs = {
            "xqT": sbG("xqT2", [128, 8 * SUP], BF16),
            "hqT": sbG("hqT2", [128, NIC * SUP], BF16),
            "h": sbG("h2", [128, TPS * IFF], BF16),
            "wg": sbG("wg2", [128, 2 * 8 * WSH], BF16),
            "wu": sbG("wu2", [128, 2 * 8 * WSH], BF16),
            "wd": sbG("wd2", [128, NIC * 1024], BF16),
        }
        for sti in range(NSUP):
            @Sy
            def _(sy, sti=sti):
                SM["x1"].wait(sy)
                SM["sq"].wait(sy)
                SM["st"].wait(sy)
                SM["outd"].wait(sy)
                SM["ld"].dma(sy.dma_start(
                    out=x_st2[:].rearrange("p (t d) -> p t d", t=TPS),
                    in_=x2_sp.ap().rearrange("(t p) d -> p t d", p=128)
                    [:, sti * TPS:(sti + 1) * TPS, :]))
            ld_rec = SM["ld"].n

            def out_write(tl, sti=sti):
                @Sy
                def _(sy, tl=tl, sti=sti):
                    SM["x1"].wait(sy)
                    tok = sti * SUP + tl * 128
                    SM["outd"].dma(sy.dma_start(
                        out=out_d[tok:tok + 128, :],
                        in_=x_st2[:, tl * 1024:(tl + 1) * 1024]))

            emit_ffn(x_st2[:], TPS, sti * SUP, f2bufs, ln2_spill=False,
                     out_write=out_write, x_ready=(("ld", ld_rec),))
        stP6.close()

        @Sy
        def _(sy):
            SM["outd"].wait(sy)

    return nc


_NC_CACHE = {}


def _get_nc(T):
    if T not in _NC_CACHE:
        _NC_CACHE[T] = build_nc(T)
    return _NC_CACHE[T]


def make_in_maps(x, Wq, bq, Wk, bk, Wv, bv, Wo, bo, w_gate, w_up, w_down, T):
    f = np.float32
    x = np.ascontiguousarray(np.asarray(x, dtype=f))
    perm = np.concatenate([np.arange(0, DH, 2), np.arange(1, DH, 2)])
    wqT = np.ascontiguousarray(np.asarray(Wq, f)[perm].T)
    wkT = np.ascontiguousarray(np.asarray(Wk, f)[perm].T)
    wvT = np.ascontiguousarray(np.asarray(Wv, f)[perm].T)
    woT = np.ascontiguousarray(np.asarray(Wo, f)[:, perm].T)
    bq_p = np.ascontiguousarray(np.asarray(bq, f)[perm].reshape(1, DH))
    bk_p = np.ascontiguousarray(np.asarray(bk, f)[perm].reshape(1, DH))
    bv_p = np.ascontiguousarray(np.asarray(bv, f)[perm].reshape(1, DH))
    bo_r = np.ascontiguousarray(np.asarray(bo, f).reshape(1, D))
    wg = np.asarray(w_gate, f)
    wu = np.asarray(w_up, f)
    wd = np.asarray(w_down, f)
    ident = np.eye(128, dtype=ml_dtypes.bfloat16)
    in_maps = []
    for c in range(8):
        seq, sl = c // 4, c % 4
        gsh = np.zeros((WSH, D), f)
        ush = np.zeros((WSH, D), f)
        dsh = np.zeros((D, WSH), f)
        lo = c * WSH
        hi = min((c + 1) * WSH, IFF)
        if hi > lo:
            gsh[0:hi - lo] = wg[lo:hi]
            ush[0:hi - lo] = wu[lo:hi]
            dsh[:, 0:hi - lo] = wd[:, lo:hi]
        in_maps.append({
            "x_own": np.ascontiguousarray(x[seq, sl * T:(sl + 1) * T]),
            "wqT": wqT, "wkT": wkT, "wvT": wvT, "woT": woT,
            "bq": bq_p, "bk": bk_p, "bv": bv_p, "bo": bo_r,
            "gate_sh": gsh, "up_sh": ush, "down_sh": dsh,
            "ident": ident,
        })
    return in_maps


def kernel(L0_attn_hidden_states, L0_attn_Wq, L0_attn_bq, L0_attn_Wk,
           L0_attn_bk, L0_attn_Wv, L0_attn_bv, L0_attn_Wo, L0_attn_bo,
           L0_ffn_w_gate, L0_ffn_w_up, L0_ffn_w_down):
    x = np.asarray(L0_attn_hidden_states, dtype=np.float32)
    B, S, _ = x.shape
    T = S // 4
    nc = _get_nc(T)
    in_maps = make_in_maps(
        x, L0_attn_Wq, L0_attn_bq, L0_attn_Wk, L0_attn_bk, L0_attn_Wv,
        L0_attn_bv, L0_attn_Wo, L0_attn_bo, L0_ffn_w_gate, L0_ffn_w_up,
        L0_ffn_w_down, T)
    res = run_bass_kernel_spmd(nc, in_maps, core_ids=list(range(8)))
    out = np.empty((B, S, D), dtype=np.float32)
    for c in range(8):
        out[c // 4, (c % 4) * T:(c % 4 + 1) * T] = res.results[c]["out"]
    return out
